# version 18
# baseline (speedup 1.0000x reference)
import math
import numpy as np

import concourse.bass as bass
import concourse.mybir as mybir
from concourse.bass_utils import run_bass_kernel_spmd

F32 = mybir.dt.float32
BF16 = mybir.dt.bfloat16
AF = mybir.ActivationFunctionType
ALU = mybir.AluOpType

LN_EPS = 1e-5
EXP_SHIFT = -12.0
INV_SQRT_D = 1.0 / math.sqrt(128.0)
NEG_LN1E4_64 = -math.log(10000.0) / 64.0

ENGS = ("sync", "tensor", "vector", "scalar", "gpsimd")


class Cfg:
    def __init__(self, T=8192, E=2048, I=5632, NC=8):
        self.T, self.E, self.I, self.NC = T, E, I, NC
        self.D = 128
        self.HPC = 2
        self.HD = self.NC * self.HPC * self.D
        self.HC = self.HD // 128
        self.To = T // NC
        self.TTA = min(256, T)
        self.TQ = min(512, T)
        self.TF = min(512, self.To)
        self.EC = E // 128
        self.IC = I // 128
        assert T % self.TTA == 0 and T % self.TQ == 0 and self.To % self.TF == 0
        assert self.TQ % 128 == 0 and E % 128 == 0 and I % 128 == 0


class Sched:
    def __init__(self, nc):
        self.nc = nc
        self.q = {k: [] for k in ENGS}
        self._sems = {}
        self.cnt = {}
        self.marks = []

    def sem(self, name):
        if name not in self._sems:
            self._sems[name] = self.nc.alloc_semaphore(name)
            self.cnt[name] = 0
        return self._sems[name]

    def val(self, name):
        self.sem(name)
        return self.cnt[name]

    def wait(self, eng, name, val=None):
        h = self.sem(name)
        v = self.cnt[name] if val is None else val
        if v <= 0:
            return
        self.q[eng].append(lambda e, h=h, v=v: e.wait_ge(h, v))

    def run(self, eng, fn, inc=None, n=1):
        h = None
        if inc is not None:
            h = self.sem(inc)
            self.cnt[inc] += n
        def g(e, fn=fn, h=h, n=n):
            ins = fn(e)
            if h is not None:
                ins.then_inc(h, n)
        self.q[eng].append(g)

    def mark(self, name):
        self.marks.append((name, {k: len(self.q[k]) for k in ENGS}))

    def replay(self, eng_name, engine):
        lst = self.q[eng_name]
        bounds = [(nm, idx[eng_name]) for nm, idx in self.marks] + [("end", len(lst))]
        pos = 0
        for i, (nm, start) in enumerate(bounds[:-1]):
            end = bounds[i + 1][1]
            if end > start:
                with self.nc.named_scope(nm):
                    for f in lst[start:end]:
                        f(engine)
            pos = end

    def barrier(self, name):
        h = self.sem(name)
        target = self.cnt[name] + len(ENGS)
        for eng in ENGS:
            self.q[eng].append(lambda e, h=h: e.sem_inc(h, 1))
        self.cnt[name] = target
        for eng in ENGS:
            self.q[eng].append(lambda e, h=h, t=target: e.wait_ge(h, t))


def _r32(n):
    return (n + 31) // 32 * 32


def build_program(cfg: Cfg):
    nc = bass.Bass("TRN2", target_bir_lowering=False)
    T, E, I = cfg.T, cfg.E, cfg.I
    EC, IC, HC = cfg.EC, cfg.IC, cfg.HC
    TTA, TQ, TF, To = cfg.TTA, cfg.TQ, cfg.TF, cfg.To
    NTA, NQ, NF = T // TTA, T // TQ, To // TF
    KPT = TTA // 128
    W3 = 3 * cfg.D

    x_fm = nc.dram_tensor("x_fm", [E, T], F32, kind="ExternalInput").ap()
    x_own = nc.dram_tensor("x_own", [E, To], F32, kind="ExternalInput").ap()
    wqkvT = nc.dram_tensor("wqkvT", [E, cfg.HPC * W3], F32, kind="ExternalInput").ap()
    wprojT = nc.dram_tensor("wprojT", [cfg.HD, E], F32, kind="ExternalInput").ap()
    gateT = nc.dram_tensor("gateT", [E, I], F32, kind="ExternalInput").ap()
    upT = nc.dram_tensor("upT", [E, I], F32, kind="ExternalInput").ap()
    downT = nc.dram_tensor("downT", [I, E], F32, kind="ExternalInput").ap()
    gate_s = nc.dram_tensor("gate_s", [128, IC], F32, kind="ExternalInput").ap()
    up_s = nc.dram_tensor("up_s", [128, IC], F32, kind="ExternalInput").ap()
    down_s = nc.dram_tensor("down_s", [128, EC], F32, kind="ExternalInput").ap()
    out_fm = nc.dram_tensor("out_fm", [E, To], F32, kind="ExternalOutput").ap()

    ct_d = nc.dram_tensor("ct_d", [128, T], F32).ap()
    st_d = nc.dram_tensor("st_d", [128, T], F32).ap()
    ln1_hi_d = nc.dram_tensor("ln1_hi_d", [128, EC * T], BF16).ap()
    ln1_lo_d = nc.dram_tensor("ln1_lo_d", [128, EC * T], BF16).ap()
    qsp_hi = nc.dram_tensor("qsp_hi", [128, T], BF16).ap()
    qsp_lo = nc.dram_tensor("qsp_lo", [128, T], BF16).ap()
    agin = [nc.dram_tensor(f"agin{h}", [128, T], F32) for h in range(cfg.HPC)]
    agout = [nc.dram_tensor(f"agout{h}", [cfg.NC * 128, T], F32) for h in range(cfg.HPC)]

    SBUF_BYTES = 212800
    base0, _ = nc.bump_sbuf(SBUF_BYTES, side="left", align_bytes=32)
    cur = [0]

    def at(name, shape, dtype, o=None):
        if o is None:
            o = cur[0]
            cur[0] = o + _r32(int(np.prod(shape[1:])) * mybir.dt.size(dtype))
        return nc.alloc_sbuf_tensor_at(name, list(shape), dtype, offset=base0 + o)

    ones_col_bf = at("ones_col_bf", [128, 2], BF16)
    ones_row_f32 = at("ones_row_f32", [1, 128], F32)
    ones_col_f32 = at("ones_col_f32", [128, 1], F32)
    swapmat = at("swapmat", [128, 128], F32)
    ident = at("ident", [128, 128], F32)
    invf_col = at("invf_col", [128, 1], F32)
    pih_col = at("pih_col", [128, 1], F32)
    sgn_col = at("sgn_col", [128, 1], F32)
    shift_col = at("shift_col", [128, 1], F32)
    tmp_col = at("tmp_col", [128, 4], F32)
    gs_cols = at("gs_cols", [128, IC], F32)
    us_cols = at("us_cols", [128, IC], F32)
    ds_cols = at("ds_cols", [128, EC], F32)
    rows = at("rows", [1, 4 * 512], F32)
    PBASE = cur[0]
    assert PBASE < 16 * 1024, PBASE

    cur[0] = PBASE
    k_hi = at("k_hi", [128, T], BF16)
    k_lo = at("k_lo", [128, T], BF16)
    v_hi = at("v_hi", [128, T], BF16)
    v_lo = at("v_lo", [128, T], BF16)
    ASTREAM = cur[0]
    XA0OFF = cur[0]
    xa0 = at("xa", [128, EC * TTA], F32)
    lhA = at("lhA", [128, EC * 2 * TTA], BF16)
    llA = at("llA", [128, EC * 2 * TTA], BF16)
    lh = [lhA[:, 0:EC * TTA], lhA[:, EC * TTA:]]
    ll = [llA[:, 0:EC * TTA], llA[:, EC * TTA:]]
    SCOFF = cur[0]
    sc_f32 = at("sc_f32", [128, EC * TTA], F32)
    XA1OFF = cur[0]
    xa1 = at("xa1", [128, EC * TTA], F32)
    wq_hi = at("wq_hi", [128, EC * W3], BF16)
    wq_lo = at("wq_lo", [128, EC * W3], BF16)
    PAIR = 2 * TTA
    ACL = cur[0]
    ep_f32 = at("ep_f32", [128, PAIR], F32)
    rp_t1 = at("rp_t1", [128, PAIR], F32)
    rp_t2 = at("rp_t2", [128, PAIR], F32)
    cttile = [at(f"cttile{b}", [128, PAIR], F32) for b in range(2)]
    sttile = [at(f"sttile{b}", [128, PAIR], F32) for b in range(2)]
    sp_hi = at("sp_hi", [128, PAIR], BF16)
    sp_lo = at("sp_lo", [128, PAIR], BF16)
    bR = at("bR", [128, TTA], F32)
    bM = at("bM", [128, TTA], F32)
    xh = at("xh", [128, EC * TTA], BF16, ACL)
    xl = at("xl", [128, EC * TTA], BF16, ACL + _r32(EC * TTA * 2))
    sqb = [at("sqb0", [128, EC * TTA], BF16, SCOFF),
           at("sqb1", [128, EC * TTA], BF16, SCOFF + _r32(EC * TTA * 2))]
    pbh1 = at("pbh1", [128, EC * PAIR], BF16, XA0OFF)
    pbl1 = at("pbl1", [128, EC * PAIR], BF16, XA1OFF)
    pbuf_hi = [lhA, pbh1]
    pbuf_lo = [llA, pbl1]
    AB_END = cur[0]
    cur[0] = ASTREAM
    q_hi = at("q_hi", [128, TQ], BF16)
    q_lo = at("q_lo", [128, TQ], BF16)
    p_sb = [at(f"p_sb{b}", [128, TQ], F32) for b in range(3)]
    p_hi = [at(f"p_hi{b}", [128, TQ], BF16) for b in range(3)]
    p_lo = [at(f"p_lo{b}", [128, TQ], BF16) for b in range(3)]
    o_sb = at("o_sb", [128, TQ], F32)
    rb_sb = at("rb_sb", [128, TQ], F32)
    msk = at("msk", [128, (TQ // 128) * TQ], F32)
    B_END = cur[0]
    assert B_END <= AB_END

    R0 = PBASE
    R1 = R0 + _r32(HC * TF * 4)
    R2 = R1 + _r32(max(HC, EC) * TF * 4)
    R3 = R2 + _r32(EC * TF * 4)
    R3SZ = max(2 * _r32(HC * 128 * 4) + 4 * _r32(HC * 128 * 2),
               0)
    HBYTES = 2 * _r32(IC * TF * 2)
    R5 = R3 + max(R3SZ, HBYTES)
    y_f32 = at("y_f32", [128, HC * TF], F32, R0)
    ao = at("ao", [128, EC * TF], F32, R0)
    gu_bf = [at(f"gu_bf{b}", [128, 2 * EC * 128], BF16,
                R0 + b * _r32(2 * EC * 128 * 2)) for b in range(2)]
    dn_bf = [at(f"dn_bf{b}", [128, IC * 128], BF16,
                R0 + b * _r32(IC * 128 * 2)) for b in range(2)]
    y_hi = at("y_hi", [128, HC * TF], BF16, R1)
    y_lo = at("y_lo", [128, HC * TF], BF16, R1 + _r32(HC * TF * 2))
    x2h = at("x2h", [128, EC * TF], BF16, R1)
    x2l = at("x2l", [128, EC * TF], BF16, R1 + _r32(EC * TF * 2))
    oo = at("oo", [128, EC * TF], F32, R1)
    x2 = at("x2", [128, EC * TF], F32, R2)
    wp_f32 = [at(f"wp_f32{b}", [128, HC * 128], F32,
                 R3 + b * _r32(HC * 128 * 4)) for b in range(2)]
    _wpb = R3 + 2 * _r32(HC * 128 * 4)
    wp_hi = [at(f"wp_hi{b}", [128, HC * 128], BF16,
                _wpb + b * _r32(HC * 128 * 2)) for b in range(2)]
    wp_lo = [at(f"wp_lo{b}", [128, HC * 128], BF16,
                _wpb + (2 + b) * _r32(HC * 128 * 2)) for b in range(2)]
    h_hi = at("h_hi", [128, IC * TF], BF16, R3)
    h_lo = at("h_lo", [128, IC * TF], BF16, R3 + _r32(IC * TF * 2))
    cs1 = at("cs1", [128, TF], F32, R5)
    cs2 = at("cs2", [128, TF], F32, R5 + _r32(TF * 4))
    bCR = at("bCR", [128, TF], F32, R5 + 2 * _r32(TF * 4))
    bCM = at("bCM", [128, TF], F32, R5 + 3 * _r32(TF * 4))
    csA = [cs1, at("cs3", [128, TF], F32, R5 + 4 * _r32(TF * 4))]
    csB = [cs2, at("cs4", [128, TF], F32, R5 + 5 * _r32(TF * 4))]
    C_END = R5 + 6 * _r32(TF * 4)
    assert max(AB_END, C_END) <= SBUF_BYTES, (AB_END, C_END, SBUF_BYTES)

    ct_full = at("ct_full", [128, T], F32, PBASE)
    st_full = at("st_full", [128, T], F32, PBASE + _r32(T * 4))
    rb_t1 = at("rb_t1", [128, T], F32, PBASE + 2 * _r32(T * 4))
    rb_t2 = at("rb_t2", [128, T], F32, PBASE + 3 * _r32(T * 4))

    pb = [nc.alloc_psum_tensor(f"pb{i}", [128, 512], F32) for i in range(8)]

    s = Sched(nc)

    def row(i, n):
        return rows[0:1, i * 512:i * 512 + n]

    def phase0():
        s.run("gpsimd", lambda g: g.memset(ones_col_bf[:, :], 1.0), inc="g0")
        s.run("gpsimd", lambda g: g.memset(ones_row_f32[:, :], 1.0), inc="g0")
        s.run("gpsimd", lambda g: g.memset(ones_col_f32[:, :], 1.0), inc="g0")
        s.run("gpsimd", lambda g: g.memset(ident[:, :], 1.0), inc="g0")
        s.run("gpsimd", lambda g: g.memset(swapmat[:, :], 1.0), inc="g0")
        s.run("gpsimd", lambda g: g.memset(sgn_col[:, :], 1.0), inc="g0")
        s.run("gpsimd", lambda g: g.memset(shift_col[:, :], EXP_SHIFT), inc="g0")
        s.run("gpsimd", lambda g: g.memset(ct_full[:, 0:1], 1.0), inc="g0")
        s.run("gpsimd", lambda g: g.memset(st_full[:, 0:1], 0.0), inc="g0")
        s.wait("gpsimd", "g0")
        s.run("gpsimd", lambda g: g.affine_select(
            ident[:, :], ident[:, :], pattern=[[1, 128]],
            compare_op=ALU.is_equal, fill=0.0, base=0, channel_multiplier=-1), inc="g0")
        s.run("gpsimd", lambda g: g.affine_select(
            swapmat[0:64, :], swapmat[0:64, :], pattern=[[1, 128]],
            compare_op=ALU.is_equal, fill=0.0, base=-64, channel_multiplier=-1), inc="g0")
        s.wait("gpsimd", "g0")
        s.run("gpsimd", lambda g: g.affine_select(
            swapmat[64:128, :], swapmat[64:128, :], pattern=[[1, 128]],
            compare_op=ALU.is_equal, fill=0.0, base=0, channel_multiplier=-1), inc="g0")
        s.run("gpsimd", lambda g: g.affine_select(
            sgn_col[:, :], sgn_col[:, :], pattern=[[0, 1]],
            compare_op=ALU.is_ge, fill=-1.0, base=-64, channel_multiplier=1), inc="g0")
        s.run("gpsimd", lambda g: g.iota(
            invf_col[0:64, :], pattern=[[0, 1]], base=0, channel_multiplier=1,
            allow_small_or_imprecise_dtypes=True), inc="g0")
        s.run("gpsimd", lambda g: g.iota(
            invf_col[64:128, :], pattern=[[0, 1]], base=0, channel_multiplier=1,
            allow_small_or_imprecise_dtypes=True), inc="g0")
        s.wait("scalar", "g0")
        s.run("scalar", lambda sc: sc.activation(
            invf_col[:, :], invf_col[:, :], AF.Exp, bias=0.0,
            scale=NEG_LN1E4_64), inc="s0")
        s.run("scalar", lambda sc: sc.activation(
            pih_col[:, :], ones_col_f32[:, :], AF.Arctan, bias=0.0, scale=1.0), inc="s0")
        s.wait("vector", "s0")
        s.run("vector", lambda v: v.tensor_scalar_mul(pih_col[:, :], pih_col[:, :], 2.0), inc="v0")
        s.wait("scalar", "v0")
        s.run("scalar", lambda sc: sc.activation(
            ct_full[:, 1:2], invf_col[:, :], AF.Sin, bias=pih_col[:, :], scale=-1.0), inc="s0")
        s.run("scalar", lambda sc: sc.activation(
            tmp_col[:, 0:1], invf_col[:, :], AF.Sin, bias=0.0, scale=1.0), inc="s0")
        s.wait("vector", "s0")
        s.run("vector", lambda v: v.tensor_mul(
            st_full[:, 1:2], tmp_col[:, 0:1], sgn_col[:, :]), inc="v0")
        Tc = 2
        while Tc <= T // 2:
            s.wait("vector", "v0")
            s.run("vector", lambda v, Tc=Tc: v.tensor_mul(
                tmp_col[:, 1:2], ct_full[:, Tc - 1:Tc], ct_full[:, 1:2]), inc="v0")
            s.run("vector", lambda v, Tc=Tc: v.tensor_mul(
                tmp_col[:, 2:3], st_full[:, Tc - 1:Tc], st_full[:, 1:2]), inc="v0")
            s.run("vector", lambda v, Tc=Tc: v.tensor_mul(
                tmp_col[:, 3:4], st_full[:, Tc - 1:Tc], ct_full[:, 1:2]), inc="v0")
            s.run("vector", lambda v, Tc=Tc: v.tensor_mul(
                tmp_col[:, 0:1], ct_full[:, Tc - 1:Tc], st_full[:, 1:2]), inc="v0")
            s.wait("vector", "v0")
            s.run("vector", lambda v, Tc=Tc: v.tensor_sub(
                ct_full[:, Tc:Tc + 1], tmp_col[:, 1:2], tmp_col[:, 2:3]), inc="v0")
            s.run("vector", lambda v, Tc=Tc: v.tensor_add(
                st_full[:, Tc:Tc + 1], tmp_col[:, 3:4], tmp_col[:, 0:1]), inc="v0")
            s.wait("vector", "v0")
            Lc = min(Tc, T - Tc)
            if Lc > 1:
                s.run("vector", lambda v, Tc=Tc, Lc=Lc: v.tensor_scalar(
                    rb_t1[:, 1:Lc], st_full[:, 1:Lc], st_full[:, Tc:Tc + 1],
                    None, ALU.mult), inc="v0")
                s.run("vector", lambda v, Tc=Tc, Lc=Lc: v.tensor_scalar(
                    rb_t2[:, 1:Lc], ct_full[:, 1:Lc], st_full[:, Tc:Tc + 1],
                    None, ALU.mult), inc="v0")
                s.wait("vector", "v0")
                s.run("vector", lambda v, Tc=Tc, Lc=Lc: v.scalar_tensor_tensor(
                    ct_full[:, Tc + 1:Tc + Lc], ct_full[:, 1:Lc],
                    ct_full[:, Tc:Tc + 1], rb_t1[:, 1:Lc],
                    ALU.mult, ALU.subtract), inc="v0")
                s.run("vector", lambda v, Tc=Tc, Lc=Lc: v.scalar_tensor_tensor(
                    st_full[:, Tc + 1:Tc + Lc], st_full[:, 1:Lc],
                    ct_full[:, Tc:Tc + 1], rb_t2[:, 1:Lc],
                    ALU.mult, ALU.add), inc="v0")
                s.wait("vector", "v0")
            Tc *= 2
        s.run("vector", lambda v: v.engine_nop(), inc="v0")
        s.wait("sync", "v0")
        s.run("sync", lambda e: e.dma_start(out=ct_d, in_=ct_full[:, :]), inc="dsy", n=16)
        s.run("sync", lambda e: e.dma_start(out=st_d, in_=st_full[:, :]), inc="dsy", n=16)
        s.run("sync", lambda e: e.dma_start(out=gs_cols[:, :], in_=gate_s), inc="dsy", n=16)
        s.run("sync", lambda e: e.dma_start(out=us_cols[:, :], in_=up_s), inc="dsy", n=16)
        s.run("sync", lambda e: e.dma_start(out=ds_cols[:, :], in_=down_s), inc="dsy", n=16)
        s.wait("sync", "dsy")

    def phaseA0():
        xab = [xa0, xa1]
        xhab = [xh, xl]
        stbanks = [(pb[4], pb[5]), (pb[2], pb[3])]
        assert 2 * TTA <= 512
        def rowh(i, half):
            return rows[0:1, i * 512 + half * TTA: i * 512 + half * TTA + TTA]

        def head(tt):
            b = tt % 2
            t0 = tt * TTA
            xa = xab[b]
            xhb = xhab[b]
            sums, sumsq = stbanks[b]
            if tt >= 2:
                s.wait("sync", "v_ln", s.xa_done.get(tt - 2, 0))
            s.run("sync", lambda e, t0=t0, xa=xa: e.dma_start(
                out=xa[:, :],
                in_=x_fm.rearrange("(ec p) t -> p ec t", p=128)[:, :, t0:t0 + TTA]),
                inc=f"d_x{b}", n=16)
            d_x_now = s.val(f"d_x{b}")
            s.wait("scalar", f"d_x{b}", d_x_now)
            if tt >= 2:
                s.wait("scalar", "t_st", s.xa_tdone.get(tt - 2, 0))
                s.wait("scalar", "t_sq2", s.sq_done.get(tt - 2, 0))
            s.run("scalar", lambda sc, xa=xa, xhb=xhb: sc.activation(
                xhb[:, :], xa[:, :], AF.Copy, bias=0.0, scale=1.0), inc="s_xc")
            s.run("scalar", lambda sc, xa=xa, b=b: sc.activation(
                sqb[b][:, :], xa[:, :], AF.Square, bias=0.0, scale=1.0), inc="s_xc")
            s_xc_now = s.val("s_xc")
            s.wait("tensor", "s_xc", s_xc_now)
            if tt >= 2:
                s.wait("tensor", "v_rw", s.rw_done.get(tt - 2, 0))
            nmm = 0
            for ec in range(EC):
                s.run("tensor", lambda t, ec=ec, sums=sums, xhb=xhb,
                      first=(nmm == 0), last=(nmm == EC - 1): t.matmul(
                    sums[0:1, 0:TTA], ones_col_bf[:, 0:1],
                    xhb[:, ec * TTA:(ec + 1) * TTA], start=first, stop=last),
                    inc="t_st" if nmm == EC - 1 else None)
                nmm += 1
            s.xa_tdone[tt] = s.val("t_st")
            nmm = 0
            for ec in range(EC):
                s.run("tensor", lambda t, ec=ec, sumsq=sumsq, b=b,
                      first=(nmm == 0), last=(nmm == EC - 1): t.matmul(
                    sumsq[0:1, 0:TTA], ones_col_bf[:, 0:1],
                    sqb[b][:, ec * TTA:(ec + 1) * TTA], start=first, stop=last),
                    inc="t_sq2" if nmm == EC - 1 else None)
                nmm += 1
            s.sq_done[tt] = s.val("t_sq2")
            t_sq_now = s.val("t_sq2")
            s.wait("vector", "t_st", s.xa_tdone[tt])
            s.wait("vector", "t_sq2", t_sq_now)
            if tt >= 2:
                s.wait("vector", "t_bc", s.bc_done.get(tt - 2, 0))
            s.run("vector", lambda v, b=b, sums=sums: v.tensor_scalar_mul(
                rowh(0, b), sums[0:1, 0:TTA], 1.0 / E), inc="v_rw")
            s.run("vector", lambda v, b=b, sumsq=sumsq: v.tensor_scalar_mul(
                rowh(1, b), sumsq[0:1, 0:TTA], 1.0 / E), inc="v_rw")
            s.wait("vector", "v_rw")
            s.run("vector", lambda v, b=b: v.tensor_mul(rowh(2, b), rowh(0, b), rowh(0, b)), inc="v_rw")
            s.wait("vector", "v_rw")
            s.run("vector", lambda v, b=b: v.tensor_sub(rowh(3, b), rowh(1, b), rowh(2, b)), inc="v_rw")
            s.wait("vector", "v_rw")
            s.run("vector", lambda v, b=b: v.tensor_scalar_add(rowh(3, b), rowh(3, b), LN_EPS), inc="v_rw")
            s.wait("vector", "v_rw")
            s.run("vector", lambda v, b=b: v.reciprocal(rowh(2, b), rowh(3, b)), inc="v_rw")
            s.wait("scalar", "v_rw")
            s.run("scalar", lambda sc, b=b: sc.activation(
                rowh(2, b), rowh(2, b), AF.Sqrt, bias=0.0, scale=1.0), inc="s_rw")
            s.wait("vector", "s_rw")
            s.run("vector", lambda v, b=b: v.tensor_mul(rowh(3, b), rowh(0, b), rowh(2, b)), inc="v_rw")
            s.wait("vector", "v_rw")
            s.run("vector", lambda v, b=b: v.tensor_scalar_mul(rowh(3, b), rowh(3, b), -1.0), inc="v_rw")
            s.rw_done[tt] = s.val("v_rw")

        def tail(tt):
            b = tt % 2
            t0 = tt * TTA
            xa = xab[b]
            s.wait("tensor", "v_rw", s.rw_done[tt])
            if tt >= 1:
                s.wait("tensor", "v_bc", s.vbc_done.get(tt - 1, 0))
            s.run("tensor", lambda t, b=b: t.matmul(
                pb[6][0:128, 0:TTA], ones_row_f32[:, :], rowh(2, b),
                start=True, stop=True), inc="t_bc")
            s.run("tensor", lambda t, b=b: t.matmul(
                pb[7][0:128, 0:TTA], ones_row_f32[:, :], rowh(3, b),
                start=True, stop=True), inc="t_bc")
            s.bc_done[tt] = s.val("t_bc")
            s.wait("vector", "t_bc", s.bc_done[tt])
            s.run("vector", lambda v: v.tensor_copy(bR[:, :], pb[6][0:128, 0:TTA]), inc="v_bc")
            s.run("vector", lambda v: v.tensor_copy(bM[:, :], pb[7][0:128, 0:TTA]), inc="v_bc")
            s.vbc_done[tt] = s.val("v_bc")
            s.wait("vector", "v_bc")
            if tt >= 2:
                s.wait("vector", f"d_ls{tt % 2}")
            x3 = lambda t_, xa=xa: xa[:, :].rearrange("p (ec t) -> p ec t", t=TTA)
            s.run("vector", lambda v, xa=xa: v.tensor_tensor(
                xa[:, :].rearrange("p (ec t) -> p ec t", t=TTA),
                xa[:, :].rearrange("p (ec t) -> p ec t", t=TTA),
                bR[:, :].rearrange("p (a t) -> p a t", a=1).broadcast_to((128, EC, TTA)), ALU.mult), inc="v_ln")
            s.wait("vector", "v_ln")
            s.run("vector", lambda v, xa=xa: v.tensor_tensor(
                xa[:, :].rearrange("p (ec t) -> p ec t", t=TTA),
                xa[:, :].rearrange("p (ec t) -> p ec t", t=TTA),
                bM[:, :].rearrange("p (a t) -> p a t", a=1).broadcast_to((128, EC, TTA)), ALU.add), inc="v_ln")
            s.wait("scalar", "v_ln")
            s.run("scalar", lambda sc, b=b, xa=xa: sc.activation(
                lh[b][:, :], xa[:, :], AF.Copy, bias=0.0, scale=1.0), inc="s_lh")
            s_lh_now = s.val("s_lh")
            s.wait("vector", "s_lh", s_lh_now)
            s.run("vector", lambda v, b=b, xa=xa: v.tensor_tensor(
                ll[b][:, :], xa[:, :], lh[b][:, :], ALU.subtract), inc="v_ln")
            ln_now = s.val("v_ln")
            s.xa_done[tt] = ln_now
            s.wait("scalar", "v_ln", ln_now)
            s.run("scalar", lambda e, b=b, t0=t0: e.dma_start(
                out=ln1_hi_d.rearrange("p (ec t) -> p ec t", t=T)[:, :, t0:t0 + TTA],
                in_=lh[b][:, :]), inc=f"d_ls{tt % 2}", n=16)
            s.run("scalar", lambda e, b=b, t0=t0: e.dma_start(
                out=ln1_lo_d.rearrange("p (ec t) -> p ec t", t=T)[:, :, t0:t0 + TTA],
                in_=ll[b][:, :]), inc=f"d_ls{tt % 2}", n=16)

        for tt in range(NTA + 1):
            if tt < NTA:
                head(tt)
            if tt >= 1:
                tail(tt - 1)
        for nm in ("d_x0", "d_x1", "dsy"):
            s.wait("sync", nm)
        s.wait("sync", "t_st", s.val("t_st"))
        s.wait("sync", "t_sq2", s.val("t_sq2"))
        for nm in ("d_ls0", "d_ls1"):
            s.wait("scalar", nm)

    def phaseA(hp):
        HEC = max(1, EC // 2)
        for half in range((EC + HEC - 1) // HEC):
            e0, e1 = half * HEC, min((half + 1) * HEC, EC)
            ne = e1 - e0
            s.wait("sync", "v_wq", s.val("v_wq"))
            s.run("sync", lambda e, hp=hp, e0=e0, e1=e1, ne=ne: e.dma_start(
                out=sc_f32[:, 0:ne * W3],
                in_=wqkvT.rearrange("(ec p) f -> p ec f", p=128)[
                    :, e0:e1, hp * W3:(hp + 1) * W3]),
                inc="d_wq", n=16)
            s.wait("vector", "d_wq")
            s.wait("vector", "t_qkv", s.val("t_qkv"))
            s.run("vector", lambda v, e0=e0, ne=ne: v.tensor_copy(
                wq_hi[:, e0 * W3:(e0 + ne) * W3], sc_f32[:, 0:ne * W3]), inc="v_wq")
            s.wait("vector", "v_wq")
            s.run("vector", lambda v, e0=e0, ne=ne: v.tensor_tensor(
                wq_lo[:, e0 * W3:(e0 + ne) * W3], sc_f32[:, 0:ne * W3],
                wq_hi[:, e0 * W3:(e0 + ne) * W3], ALU.subtract), inc="v_wq")

        assert NTA % 2 == 0
        NP = NTA // 2
        KPP = PAIR // 128
        qbanks = [pb[0], pb[1], pb[4], pb[5]]

        def emit_mm(u):
            pp, fc = u
            b = pp % 2
            t0 = pp * PAIR
            bh, bl = pbuf_hi[b], pbuf_lo[b]
            if fc == 0:
                if pp >= 2:
                    s.wait("sync", "t_qkv", s.tile_mm_done[(hp, pp - 2)])
                    s.wait("sync", "v_ep", s.rope_done.get((hp, pp - 2), 0))
                s.run("sync", lambda e, bh=bh, t0=t0: e.dma_start(
                    out=bh[:, :].rearrange("p (ec t) -> p ec t", t=PAIR),
                    in_=ln1_hi_d.rearrange("p (ec t) -> p ec t", t=T)[:, :, t0:t0 + PAIR]),
                    inc=f"d_x{b}", n=16)
                s.run("sync", lambda e, bl=bl, t0=t0: e.dma_start(
                    out=bl[:, :].rearrange("p (ec t) -> p ec t", t=PAIR),
                    in_=ln1_lo_d.rearrange("p (ec t) -> p ec t", t=T)[:, :, t0:t0 + PAIR]),
                    inc=f"d_x{b}", n=16)
                s.run("sync", lambda e, b=b, t0=t0: e.dma_start(
                    out=cttile[b][:, :], in_=ct_d[:, t0:t0 + PAIR]), inc=f"d_x{b}", n=16)
                s.run("sync", lambda e, b=b, t0=t0: e.dma_start(
                    out=sttile[b][:, :], in_=st_d[:, t0:t0 + PAIR]), inc=f"d_x{b}", n=16)
                s.pair_ld[pp] = s.val(f"d_x{b}")
            pbank = qbanks[(3 * pp + fc) % 4]
            s.wait("tensor", f"d_x{b}", s.pair_ld[pp])
            if fc == 0:
                s.wait("tensor", "v_wq")
            key = ("ab", (3 * pp + fc) % 4)
            if key in s.bank_free:
                s.wait("tensor", s.bank_free[key][0], s.bank_free[key][1])
            nmm = 0
            NMM = 3 * EC
            for ec in range(EC):
                for (wsrc, xsrc) in ((wq_hi, bh), (wq_hi, bl), (wq_lo, bh)):
                    s.run("tensor", lambda t, wsrc=wsrc, xsrc=xsrc, ec=ec, fc=fc,
                          pbank=pbank, first=(nmm == 0), last=(nmm == NMM - 1): t.matmul(
                        pbank[0:128, 0:PAIR],
                        wsrc[:, ec * W3 + fc * 128:ec * W3 + fc * 128 + 128],
                        xsrc[:, ec * PAIR:(ec + 1) * PAIR],
                        start=first, stop=last),
                        inc="t_qkv" if nmm == NMM - 1 else None)
                    nmm += 1
            s.unit_mm[u] = s.val("t_qkv")
            if fc == 2:
                s.tile_mm_done[(hp, pp)] = s.val("t_qkv")

        def emit_epi(u):
            pp, fc = u
            b = pp % 2
            t0 = pp * PAIR
            pbank = qbanks[(3 * pp + fc) % 4]
            s.wait("vector", "t_qkv", s.unit_mm[u])
            if s.ep_free is not None:
                s.wait("vector", s.ep_free[0], s.ep_free[1])
            s.run("vector", lambda v, pbank=pbank: v.tensor_copy(
                ep_f32[:, :], pbank[0:128, 0:PAIR]), inc="v_ep")
            v_ep_now = s.val("v_ep")
            s.bank_free[("ab", (3 * pp + fc) % 4)] = ("v_ep", v_ep_now)

            if fc < 2:
                s.wait("tensor", "v_ep", v_ep_now)
                if s.pb7_free is not None:
                    s.wait("tensor", s.pb7_free[0], s.pb7_free[1])
                s.run("tensor", lambda t: t.matmul(
                    pb[7][0:128, 0:PAIR], swapmat[:, :], ep_f32[:, :],
                    start=True, stop=True), inc="t_rp")
                t_rp_now = s.val("t_rp")
                s.wait("vector", "t_rp", t_rp_now)
                s.wait("vector", f"d_x{b}", s.pair_ld[pp])
                s.run("vector", lambda v, b=b: v.tensor_mul(
                    rp_t1[:, :], ep_f32[:, :], cttile[b][:, :]), inc="v_ep")
                s.run("vector", lambda v, b=b: v.tensor_mul(
                    rp_t2[:, :], pb[7][0:128, 0:PAIR], sttile[b][:, :]), inc="v_ep")
                s.wait("vector", "v_ep")
                s.pb7_free = ("v_ep", s.val("v_ep"))
                s.run("vector", lambda v: v.tensor_add(
                    ep_f32[:, :], rp_t1[:, :], rp_t2[:, :]), inc="v_ep")
                s.wait("vector", "v_ep")

            if fc == 0:
                if s.sp_free is not None:
                    s.wait("vector", s.sp_free[0], s.sp_free[1])
                s.run("vector", lambda v: v.tensor_copy(sp_hi[:, :], ep_f32[:, :]), inc="v_ep")
                s.wait("vector", "v_ep")
                s.run("vector", lambda v: v.tensor_tensor(
                    sp_lo[:, :], ep_f32[:, :], sp_hi[:, :], ALU.subtract), inc="v_ep")
                v_ep2 = s.val("v_ep")
                s.ep_free = ("v_ep", v_ep2)
                s.wait("scalar", "v_ep", v_ep2)
                s.run("scalar", lambda e, t0=t0: e.dma_start(
                    out=qsp_hi[:, t0:t0 + PAIR], in_=sp_hi[:, :]), inc="d_sp", n=16)
                s.run("scalar", lambda e, t0=t0: e.dma_start(
                    out=qsp_lo[:, t0:t0 + PAIR], in_=sp_lo[:, :]), inc="d_sp", n=16)
                s.sp_free = ("d_sp", s.val("d_sp"))
            elif fc == 1:
                s.run("vector", lambda v, t0=t0: v.tensor_copy(
                    k_hi[:, t0:t0 + PAIR], ep_f32[:, :]), inc="v_ep")
                s.wait("vector", "v_ep")
                s.run("vector", lambda v, t0=t0: v.tensor_tensor(
                    k_lo[:, t0:t0 + PAIR], ep_f32[:, :], k_hi[:, t0:t0 + PAIR],
                    ALU.subtract), inc="v_ep")
                s.ep_free = ("v_ep", s.val("v_ep"))
            else:
                for dc in range(KPP):
                    kci = t0 // 128 + dc
                    s.wait("tensor", "v_ep", v_ep_now)
                    if s.pb7_free is not None:
                        s.wait("tensor", s.pb7_free[0], s.pb7_free[1])
                    s.run("tensor", lambda t, dc=dc: t.transpose(
                        pb[7][0:128, 0:128], ep_f32[:, dc * 128:(dc + 1) * 128],
                        ident[:, :]), inc="t_rp")
                    t_rp_now = s.val("t_rp")
                    s.wait("vector", "t_rp", t_rp_now)
                    s.run("vector", lambda v, kci=kci: v.tensor_copy(
                        v_hi[:, kci * 128:(kci + 1) * 128], pb[7][0:128, 0:128]),
                        inc="v_ep")
                    s.wait("vector", "v_ep")
                    s.run("vector", lambda v, kci=kci: v.tensor_tensor(
                        v_lo[:, kci * 128:(kci + 1) * 128], pb[7][0:128, 0:128],
                        v_hi[:, kci * 128:(kci + 1) * 128], ALU.subtract), inc="v_ep")
                    s.pb7_free = ("v_ep", s.val("v_ep"))
                s.ep_free = ("t_rp", s.val("t_rp"))
            s.rope_done[(hp, pp)] = s.val("v_ep")

        units = [(pp, fc) for pp in range(NP) for fc in range(3)]
        for i in range(len(units) + 1):
            if i < len(units):
                emit_mm(units[i])
            if i >= 1:
                emit_epi(units[i - 1])
        for nm in ("d_x0", "d_x1", "d_wq"):
            s.wait("sync", nm)
        s.wait("scalar", "d_sp")

    LAG = 2
    def phaseB(hp):
        stb = [pb[0], pb[1], pb[4], pb[5]]
        ND = TQ // 128
        s.run("gpsimd", lambda g: g.memset(msk[:, :], 1.0), inc="g_mk")
        s.wait("gpsimd", "g_mk")
        for d in range(ND):
            s.run("gpsimd", lambda g, d=d: g.affine_select(
                msk[:, d * TQ:(d + 1) * TQ], msk[:, d * TQ:(d + 1) * TQ],
                pattern=[[1, TQ]], compare_op=ALU.is_ge, fill=0.0,
                base=-d * 128, channel_multiplier=-1), inc="g_mk")
        g_mk_now = s.val("g_mk")
        s.wait("vector", "g_mk", g_mk_now)
        for qt in range(NQ):
            q0 = qt * TQ
            nk = (q0 + TQ) // 128
            s.wait("tensor", "s_exp", s.val("s_exp"))
            s.wait("sync", "t_s", s.val("t_s"))
            voo_prev = s.val("v_oo")
            s.run("sync", lambda e, q0=q0: e.dma_start(
                out=q_hi[:, :], in_=qsp_hi[:, q0:q0 + TQ]), inc="d_q", n=16)
            s.run("sync", lambda e, q0=q0: e.dma_start(
                out=q_lo[:, :], in_=qsp_lo[:, q0:q0 + TQ]), inc="d_q", n=16)
            d_q_now = s.val("d_q")
            s.wait("tensor", "d_q", d_q_now)
            exp_done, av_done, ph_done, pl_done = {}, {}, {}, {}
            for step in range(nk + LAG):
                kc = step
                if kc < nk:
                    k0 = kc * 128
                    stp = stb[kc % 4]
                    if kc >= 4:
                        s.wait("tensor", "s_exp", exp_done[kc - 4])
                    nmm = 0
                    for (ksrc, qsrc) in ((k_hi, q_hi), (k_hi, q_lo), (k_lo, q_hi)):
                        s.run("tensor", lambda t, ksrc=ksrc, qsrc=qsrc, k0=k0, stp=stp,
                              first=(nmm == 0), last=(nmm == 2): t.matmul(
                            stp[0:128, 0:TQ], ksrc[:, k0:k0 + 128], qsrc[:, :],
                            start=first, stop=last),
                            inc="t_s" if nmm == 2 else None)
                        nmm += 1
                j = step - LAG
                if j < 0 or j >= nk:
                    continue
                k0 = j * 128
                bb = j % 3
                stpj = stb[j % 4]
                s.wait("scalar", "t_s", s.ts_base + (j + 1))
                if j >= 3:
                    s.wait("scalar", "v_ph", ph_done[j - 3])
                    s.wait("scalar", "g_pl", pl_done[j - 3])
                s.run("scalar", lambda sc, stpj=stpj, bb=bb: sc.activation(
                    p_sb[bb][:, :], stpj[0:128, 0:TQ], AF.Exp,
                    bias=shift_col[:, 0:1], scale=INV_SQRT_D), inc="s_exp")
                exp_done[j] = s.val("s_exp")
                s.wait("vector", "s_exp", exp_done[j])
                if k0 >= q0:
                    d = (k0 - q0) // 128
                    s.run("vector", lambda v, bb=bb, d=d: v.tensor_mul(
                        p_sb[bb][:, :], p_sb[bb][:, :], msk[:, d * TQ:(d + 1) * TQ]),
                        inc="v_mk")
                    s.wait("vector", "v_mk")
                if j >= 3:
                    s.wait("vector", "t_av", av_done[j - 3])
                s.run("vector", lambda v, bb=bb: v.tensor_copy(
                    p_hi[bb][:, :], p_sb[bb][:, :]), inc="v_ph")
                ph_done[j] = s.val("v_ph")
                s.wait("vector", "v_ph", ph_done[j])
                s.run("vector", lambda v, bb=bb: v.tensor_tensor(
                    p_lo[bb][:, :], p_sb[bb][:, :], p_hi[bb][:, :],
                    ALU.subtract), inc="g_pl")
                pl_done[j] = s.val("g_pl")
                s.wait("tensor", "v_ph", ph_done[j])
                s.wait("tensor", "g_pl", pl_done[j])
                if j == 0:
                    s.wait("tensor", "v_oo", voo_prev)
                s.run("tensor", lambda t, bb=bb, j=j: t.matmul(
                    pb[6][0:1, 0:TQ], ones_col_bf[:, 0:1], p_hi[bb][:, :],
                    start=(j == 0), stop=False))
                s.run("tensor", lambda t, bb=bb, j=j, nk=nk: t.matmul(
                    pb[6][0:1, 0:TQ], ones_col_bf[:, 0:1], p_lo[bb][:, :],
                    start=False, stop=(j == nk - 1)), inc="t_sum")
                nmm = 0
                for (vsrc, psrc) in ((v_hi, p_hi[bb]), (v_hi, p_lo[bb]), (v_lo, p_hi[bb])):
                    s.run("tensor", lambda t, vsrc=vsrc, psrc=psrc, k0=k0, j=j, nk=nk,
                          first=(j == 0 and nmm == 0),
                          last=(j == nk - 1 and nmm == 2): t.matmul(
                        pb[7][0:128, 0:TQ], vsrc[:, k0:k0 + 128], psrc[:, :],
                        start=first, stop=last),
                        inc="t_av" if nmm == 2 else None)
                    nmm += 1
                av_done[j] = s.val("t_av")
            s.ts_base = s.val("t_s")
            t_av_fin = s.val("t_av")
            t_sum_fin = s.val("t_sum")
            s.wait("vector", "t_sum", t_sum_fin)
            s.run("vector", lambda v: v.tensor_copy(row(0, TQ), pb[6][0:1, 0:TQ]), inc="v_oo")
            s.wait("vector", "v_oo")
            s.run("vector", lambda v: v.reciprocal(row(1, TQ), row(0, TQ)), inc="v_oo")
            v_oo_now = s.val("v_oo")
            s.wait("tensor", "v_oo", v_oo_now)
            s.run("tensor", lambda t: t.matmul(
                pb[6][0:128, 0:TQ], ones_row_f32[:, :], row(1, TQ),
                start=True, stop=True), inc="t_rb")
            t_rb_now = s.val("t_rb")
            s.wait("vector", "t_rb", t_rb_now)
            s.run("vector", lambda v: v.tensor_copy(rb_sb[:, :], pb[6][0:128, 0:TQ]), inc="v_oo")
            s.wait("vector", "t_av", t_av_fin)
            s.wait("vector", "v_oo")
            if s.o_free is not None:
                s.wait("vector", s.o_free[0], s.o_free[1])
            s.run("vector", lambda v: v.tensor_mul(
                o_sb[:, :], pb[7][0:128, 0:TQ], rb_sb[:, :]), inc="v_oo")
            v_oo2 = s.val("v_oo")
            s.wait("sync", "v_oo", v_oo2)
            s.run("sync", lambda e, hp=hp, q0=q0: e.dma_start(
                out=agin[hp].ap()[:, q0:q0 + TQ], in_=o_sb[:, :]),
                inc="d_o", n=16)
            s.o_free = ("d_o", s.val("d_o"))
        for nm in ("d_o", "d_q"):
            s.wait("sync", nm)
        s.wait("gpsimd", "d_o")
        s.run("gpsimd", lambda g, hp=hp: g.collective_compute(
            "AllGather", ALU.bypass, replica_groups=[list(range(cfg.NC))],
            ins=[agin[hp].ap().opt()], outs=[agout[hp].ap().opt()]), inc="g_ag")

    def row_math_c(div):
        s.run("vector", lambda v: v.tensor_scalar_mul(row(0, TF), pb[4][0:1, 0:TF], 1.0 / div), inc="v_rw")
        s.run("vector", lambda v: v.tensor_scalar_mul(row(1, TF), pb[5][0:1, 0:TF], 1.0 / div), inc="v_rw")
        s.wait("vector", "v_rw")
        s.run("vector", lambda v: v.tensor_mul(row(2, TF), row(0, TF), row(0, TF)), inc="v_rw")
        s.wait("vector", "v_rw")
        s.run("vector", lambda v: v.tensor_sub(row(3, TF), row(1, TF), row(2, TF)), inc="v_rw")
        s.wait("vector", "v_rw")
        s.run("vector", lambda v: v.tensor_scalar_add(row(3, TF), row(3, TF), LN_EPS), inc="v_rw")
        s.wait("vector", "v_rw")
        s.run("vector", lambda v: v.reciprocal(row(2, TF), row(3, TF)), inc="v_rw")
        s.wait("scalar", "v_rw")
        s.run("scalar", lambda sc: sc.activation(
            row(2, TF), row(2, TF), AF.Sqrt, bias=0.0, scale=1.0), inc="s_rw")
        s.wait("vector", "s_rw")
        s.run("vector", lambda v: v.tensor_mul(row(3, TF), row(0, TF), row(2, TF)), inc="v_rw")
        s.wait("vector", "v_rw")
        s.run("vector", lambda v: v.tensor_scalar_mul(row(3, TF), row(3, TF), -1.0), inc="v_rw")
        v_rw_now = s.val("v_rw")
        s.wait("tensor", "v_rw", v_rw_now)
        s.run("tensor", lambda t: t.matmul(
            pb[4][0:128, 0:TF], ones_row_f32[:, :], row(2, TF), start=True, stop=True), inc="t_bc")
        s.run("tensor", lambda t: t.matmul(
            pb[5][0:128, 0:TF], ones_row_f32[:, :], row(3, TF), start=True, stop=True), inc="t_bc")
        t_bc_now = s.val("t_bc")
        s.wait("vector", "t_bc", t_bc_now)
        s.run("vector", lambda v: v.tensor_copy(bCR[:, :], pb[4][0:128, 0:TF]), inc="v_bc")
        s.run("vector", lambda v: v.tensor_copy(bCM[:, :], pb[5][0:128, 0:TF]), inc="v_bc")
        s.wait("vector", "v_bc")

    def _c_stat_chunk(srct, ec, nchunks, ready=None):
        if ready is None:
            ready = ("v_pj", s.pj_ev_done[ec])
        if ec == 0:
            s.wait("tensor", "v_bc", s.val("v_bc"))
        s.wait("tensor", ready[0], ready[1])
        s.run("tensor", lambda t, ec=ec, first=(ec == 0), last=(ec == nchunks - 1): t.matmul(
            pb[4][0:1, 0:TF], ones_col_f32[:, :], srct[:, ec * TF:(ec + 1) * TF],
            start=first, stop=last), inc="t_sc")
        buf = csA[ec % 2]
        s.wait("vector", ready[0], ready[1])
        if ec >= 2:
            s.wait("vector", "t_sq", s.csq_done.get(ec - 2, 0))
        s.run("vector", lambda v, ec=ec, buf=buf: v.tensor_mul(
            buf[:, :], srct[:, ec * TF:(ec + 1) * TF], srct[:, ec * TF:(ec + 1) * TF]),
            inc="v_sq")
        v_sq_now = s.val("v_sq")
        s.wait("tensor", "v_sq", v_sq_now)
        s.run("tensor", lambda t, ec=ec, buf=buf, first=(ec == 0), last=(ec == nchunks - 1): t.matmul(
            pb[5][0:1, 0:TF], ones_col_f32[:, :], buf[:, :],
            start=first, stop=last), inc="t_sq")
        s.csq_done[ec] = s.val("t_sq")

    def _c_stat_rows(div):
        s.wait("vector", "t_sc", s.val("t_sc"))
        s.wait("vector", "t_sq", s.val("t_sq"))
        row_math_c(div)

    def stats_c(src, nchunks, div):
        s.wait("tensor", "v_bc", s.val("v_bc"))
        for ec in range(nchunks):
            s.run("tensor", lambda t, ec=ec, first=(ec == 0), last=(ec == nchunks - 1): t.matmul(
                pb[4][0:1, 0:TF], ones_col_f32[:, :], src[:, ec * TF:(ec + 1) * TF],
                start=first, stop=last),
                inc="t_sc" if ec == nchunks - 1 else None)
        sq_done = {}
        for ec in range(nchunks):
            buf = csA[ec % 2]
            if ec >= 2:
                s.wait("vector", "t_sq", sq_done[ec - 2])
            s.run("vector", lambda v, ec=ec, buf=buf: v.tensor_mul(
                buf[:, :], src[:, ec * TF:(ec + 1) * TF], src[:, ec * TF:(ec + 1) * TF]),
                inc="v_sq")
            v_sq_now = s.val("v_sq")
            s.wait("tensor", "v_sq", v_sq_now)
            s.run("tensor", lambda t, ec=ec, buf=buf, first=(ec == 0), last=(ec == nchunks - 1): t.matmul(
                pb[5][0:1, 0:TF], ones_col_f32[:, :], buf[:, :],
                start=first, stop=last), inc="t_sq")
            sq_done[ec] = s.val("t_sq")
        s.wait("vector", "t_sc", s.val("t_sc"))
        s.wait("vector", "t_sq", s.val("t_sq"))
        row_math_c(div)

    def phaseC():
        s.barrier("ph_c")

        for tf in range(NF):
            tb = tf * TF
            if tf > 0:
                s.wait("sync", "t_pj", s.val("t_pj"))
                s.wait("sync", "t_gu", s.val("t_gu"))
                s.wait("sync", "t_dn", s.val("t_dn"))
                s.wait("vector", "t_pj", s.val("t_pj"))
                s.wait("vector", "t_dn", s.val("t_dn"))
                s.wait("vector", "t_gu", s.val("t_gu"))
                s.wait("vector", "d_out", s.val("d_out"))
                s.wait("sync", "v_ot", s.val("v_ot"))
            HC2 = HC // 2
            s.wait("sync", "g_ag", 1)
            s.run("sync", lambda e, tf=tf, HC2=HC2: e.dma_start(
                out=y_f32[:, 0:HC2 * TF],
                in_=agout[0].ap().rearrange("(hc p) t -> p hc t", p=128)[
                    :, :, bass.ts(e.bass.partition_id([mybir.EngineType.SP]) * NF + tf, TF)]),
                inc="d_y0", n=16)
            d_y0_now = s.val("d_y0")
            s.wait("sync", "g_ag", 2)
            s.run("sync", lambda e, tf=tf, HC2=HC2: e.dma_start(
                out=y_f32[:, HC2 * TF:HC * TF],
                in_=agout[1].ap().rearrange("(hc p) t -> p hc t", p=128)[
                    :, :, bass.ts(e.bass.partition_id([mybir.EngineType.SP]) * NF + tf, TF)]),
                inc="d_y", n=16)
            d_y_now = s.val("d_y")
            s.wait("vector", "d_y0", d_y0_now)
            s.run("vector", lambda v: v.tensor_copy(
                y_hi[:, 0:HC2 * TF], y_f32[:, 0:HC2 * TF]), inc="v_y0")
            s.wait("vector", "v_y0")
            s.run("vector", lambda v: v.tensor_tensor(
                y_lo[:, 0:HC2 * TF], y_f32[:, 0:HC2 * TF], y_hi[:, 0:HC2 * TF],
                ALU.subtract), inc="v_y0")
            v_y0_now = s.val("v_y0")
            s.wait("vector", "d_y", d_y_now)
            s.run("vector", lambda v: v.tensor_copy(
                y_hi[:, HC2 * TF:], y_f32[:, HC2 * TF:HC * TF]), inc="v_y")
            s.wait("vector", "v_y")
            s.run("vector", lambda v: v.tensor_tensor(
                y_lo[:, HC2 * TF:], y_f32[:, HC2 * TF:HC * TF], y_hi[:, HC2 * TF:],
                ALU.subtract), inc="v_y")
            v_y_now = s.val("v_y")
            s.run("sync", lambda e, tb=tb: e.dma_start(
                out=x2[:, :],
                in_=x_own.rearrange("(ec p) t -> p ec t", p=128)[:, :, tb:tb + TF]),
                inc="d_x2", n=16)
            d_x2_now = s.val("d_x2")

            for fc in range(EC):
                wb = fc % 2
                if fc >= 2:
                    s.wait("sync", "v_w", s.wp_split_done.get(fc - 2, 0))
                s.run("sync", lambda e, fc=fc, wb=wb: e.dma_start(
                    out=wp_f32[wb][:, :],
                    in_=wprojT.rearrange("(hc p) f -> p hc f", p=128)[:, :, fc * 128:(fc + 1) * 128]),
                    inc=f"d_w{wb}", n=16)
                d_w_now = s.val(f"d_w{wb}")
                s.wait("vector", f"d_w{wb}", d_w_now)
                if fc >= 2:
                    s.wait("vector", "t_pj", s.pj_mm_done.get((tf, fc - 2), 0))
                s.run("vector", lambda v, wb=wb: v.tensor_copy(
                    wp_hi[wb][:, :], wp_f32[wb][:, :]), inc="v_w")
                s.wait("vector", "v_w")
                s.run("vector", lambda v, wb=wb: v.tensor_tensor(
                    wp_lo[wb][:, :], wp_f32[wb][:, :], wp_hi[wb][:, :], ALU.subtract), inc="v_w")
                s.wp_split_done[fc] = s.val("v_w")
                s.wait("tensor", "v_w", s.wp_split_done[fc])
                s.wait("tensor", "v_y0", v_y0_now)
                pbank = pb[fc % 2]
                s.wait("tensor", "v_pj", s.val("v_pj"))
                nmm = 0
                NMM = 3 * HC
                for ec in range(HC):
                    if ec == HC2:
                        s.wait("tensor", "v_y", v_y_now)
                    for (wsrc, ysrc) in ((wp_hi[wb], y_hi), (wp_hi[wb], y_lo), (wp_lo[wb], y_hi)):
                        s.run("tensor", lambda t, wsrc=wsrc, ysrc=ysrc, ec=ec, pbank=pbank,
                              first=(nmm == 0), last=(nmm == NMM - 1): t.matmul(
                            pbank[0:128, 0:TF],
                            wsrc[:, ec * 128:(ec + 1) * 128],
                            ysrc[:, ec * TF:(ec + 1) * TF],
                            start=first, stop=last),
                            inc="t_pj" if nmm == NMM - 1 else None)
                        nmm += 1
                s.pj_mm_done[(tf, fc)] = s.val("t_pj")
                s.wait("vector", "t_pj", s.pj_mm_done[(tf, fc)])
                s.run("vector", lambda v, fc=fc, pbank=pbank: v.tensor_copy(
                    ao[:, fc * TF:(fc + 1) * TF], pbank[0:128, 0:TF]), inc="v_pj")
                s.pj_ev_done[fc] = s.val("v_pj")
                if fc >= 1:
                    _c_stat_chunk(ao, fc - 1, EC)
            _c_stat_chunk(ao, EC - 1, EC)
            v_pj_now = s.val("v_pj")
            s.wait("tensor", "v_pj", v_pj_now)
            s.wait("vector", "v_pj", v_pj_now)
            _c_stat_rows(E)
            s.wait("vector", "d_x2", d_x2_now)
            s.run("vector", lambda v: v.tensor_tensor(
                ao[:, 0:EC * TF].rearrange("p (ec t) -> p ec t", t=TF),
                ao[:, 0:EC * TF].rearrange("p (ec t) -> p ec t", t=TF),
                bCR[:, :].rearrange("p (a t) -> p a t", a=1).broadcast_to((128, EC, TF)), ALU.mult), inc="v_x2")
            s.wait("vector", "v_x2")
            s.run("vector", lambda v: v.tensor_tensor(
                ao[:, 0:EC * TF].rearrange("p (ec t) -> p ec t", t=TF),
                ao[:, 0:EC * TF].rearrange("p (ec t) -> p ec t", t=TF),
                bCM[:, :].rearrange("p (a t) -> p a t", a=1).broadcast_to((128, EC, TF)), ALU.add), inc="v_x2")
            s.wait("vector", "v_x2")
            s.run("vector", lambda v: v.tensor_add(
                x2[:, :], x2[:, :], ao[:, 0:EC * TF]), inc="v_x2")
            v_x2_now = s.val("v_x2")
            for ec in range(EC):
                _c_stat_chunk(x2, ec, EC, ready=("v_x2", v_x2_now))
            s.wait("tensor", "v_x2", v_x2_now)
            _c_stat_rows(E)
            s.wait("vector", "t_gu", s.val("t_gu"))
            s.run("vector", lambda v: v.tensor_tensor(
                ao[:, 0:EC * TF].rearrange("p (ec t) -> p ec t", t=TF),
                x2[:, :].rearrange("p (ec t) -> p ec t", t=TF),
                bCR[:, :].rearrange("p (a t) -> p a t", a=1).broadcast_to((128, EC, TF)), ALU.mult), inc="v_l2")
            s.wait("vector", "v_l2")
            s.run("vector", lambda v: v.tensor_tensor(
                ao[:, 0:EC * TF].rearrange("p (ec t) -> p ec t", t=TF),
                ao[:, 0:EC * TF].rearrange("p (ec t) -> p ec t", t=TF),
                bCM[:, :].rearrange("p (a t) -> p a t", a=1).broadcast_to((128, EC, TF)), ALU.add), inc="v_l2")
            s.wait("vector", "v_l2")
            s.run("vector", lambda v: v.tensor_copy(x2h[:, :], ao[:, 0:EC * TF]), inc="v_l2")
            s.wait("vector", "v_l2")
            s.run("vector", lambda v: v.tensor_tensor(
                x2l[:, :], ao[:, 0:EC * TF], x2h[:, :], ALU.subtract), inc="v_l2")
            v_l2_now = s.val("v_l2")

            s.wait("gpsimd", "v_l2", v_l2_now)
            for ic in range(IC):
                gb = ic % 2
                if ic >= 2:
                    s.wait("gpsimd", "t_gu", s.gu_mm_done.get((tf, ic - 2), 0))
                s.run("gpsimd", lambda g, ic=ic, gb=gb: g.dma_start(
                    out=gu_bf[gb][:, 0:EC * 128].rearrange("p (ec i) -> p ec i", i=128),
                    in_=gateT.rearrange("(ec p) i -> p ec i", p=128)[:, :, ic * 128:(ic + 1) * 128]),
                    inc=f"d_g{gb}", n=16)
                s.run("gpsimd", lambda g, ic=ic, gb=gb: g.dma_start(
                    out=gu_bf[gb][:, EC * 128:2 * EC * 128].rearrange("p (ec i) -> p ec i", i=128),
                    in_=upT.rearrange("(ec p) i -> p ec i", p=128)[:, :, ic * 128:(ic + 1) * 128]),
                    inc=f"d_g{gb}", n=16)
                d_g_now = s.val(f"d_g{gb}")
                s.wait("tensor", f"d_g{gb}", d_g_now)
                s.wait("tensor", "v_l2", v_l2_now)
                gpb = pb[4 + gb]
                upb = pb[0 + gb]
                if ic >= 2:
                    s.wait("tensor", "s_fe", s.fe_s_done.get((tf, ic - 2), 0))
                    s.wait("tensor", "v_fe", s.fe_v1_done.get((tf, ic - 2), 0))
                nmm = 0
                NMM = 2 * EC
                for ec in range(EC):
                    for xsrc in (x2h, x2l):
                        s.run("tensor", lambda t, ec=ec, gb=gb, xsrc=xsrc, gpb=gpb,
                              first=(nmm == 0), last=(nmm == NMM - 1): t.matmul(
                            gpb[0:128, 0:TF],
                            gu_bf[gb][:, ec * 128:(ec + 1) * 128],
                            xsrc[:, ec * TF:(ec + 1) * TF],
                            start=first, stop=last),
                            inc="t_gu" if nmm == NMM - 1 else None)
                        nmm += 1
                nmm = 0
                for ec in range(EC):
                    for xsrc in (x2h, x2l):
                        s.run("tensor", lambda t, ec=ec, gb=gb, xsrc=xsrc, upb=upb,
                              first=(nmm == 0), last=(nmm == NMM - 1): t.matmul(
                            upb[0:128, 0:TF],
                            gu_bf[gb][:, EC * 128 + ec * 128:EC * 128 + (ec + 1) * 128],
                            xsrc[:, ec * TF:(ec + 1) * TF],
                            start=first, stop=last),
                            inc="t_gu" if nmm == NMM - 1 else None)
                        nmm += 1
                s.gu_mm_done[(tf, ic)] = s.val("t_gu")
                s.wait("scalar", "t_gu", s.gu_mm_done[(tf, ic)])
                if ic >= 2:
                    s.wait("scalar", "v_fe", s.fe_v1_done.get((tf, ic - 2), 0))
                else:
                    s.wait("scalar", "t_sq", s.val("t_sq"))
                s.run("scalar", lambda sc, ic=ic, gpb=gpb, gb=gb: sc.activation(
                    csA[gb][:, :], gpb[0:128, 0:TF], AF.Silu, bias=0.0,
                    scale=gs_cols[:, ic:ic + 1]), inc="s_fe")
                s.fe_s_done[(tf, ic)] = s.val("s_fe")
                s.wait("vector", "s_fe", s.fe_s_done[(tf, ic)])
                if ic >= 2:
                    s.wait("vector", "v_fe", s.fe_v2_done.get((tf, ic - 2), 0))
                s.run("vector", lambda v, gb=gb, upb=upb: v.tensor_mul(
                    csB[gb][:, :], csA[gb][:, :], upb[0:128, 0:TF]), inc="v_fe")
                s.fe_v1_done[(tf, ic)] = s.val("v_fe")
                s.wait("vector", "v_fe")
                s.run("vector", lambda v, ic=ic, gb=gb: v.tensor_scalar(
                    csB[gb][:, :], csB[gb][:, :], us_cols[:, ic:ic + 1], None, ALU.mult), inc="v_fe")
                s.wait("vector", "v_fe")
                s.run("vector", lambda v, ic=ic, gb=gb: v.tensor_copy(
                    h_hi[:, ic * TF:(ic + 1) * TF], csB[gb][:, :]), inc="v_fe")
                s.wait("vector", "v_fe")
                s.run("vector", lambda v, ic=ic, gb=gb: v.tensor_tensor(
                    h_lo[:, ic * TF:(ic + 1) * TF], csB[gb][:, :],
                    h_hi[:, ic * TF:(ic + 1) * TF], ALU.subtract), inc="v_fe")
                s.fe_v2_done[(tf, ic)] = s.val("v_fe")
            v_fe_all = s.val("v_fe")

            s.wait("gpsimd", "t_gu", s.val("t_gu"))
            for hc in range(EC):
                db = hc % 2
                if hc >= 2:
                    s.wait("gpsimd", "t_dn", s.dn_mm_done.get((tf, hc - 2), 0))
                s.run("gpsimd", lambda g, hc=hc, db=db: g.dma_start(
                    out=dn_bf[db][:, :].rearrange("p (ic h) -> p ic h", h=128),
                    in_=downT.rearrange("(ic p) h -> p ic h", p=128)[:, :, hc * 128:(hc + 1) * 128]),
                    inc=f"d_d{db}", n=16)
                d_d_now = s.val(f"d_d{db}")
                s.wait("tensor", f"d_d{db}", d_d_now)
                s.wait("tensor", "v_fe", v_fe_all)
                pbank = pb[6 + hc % 2]
                s.wait("tensor", "v_dn", s.val("v_dn"))
                nmm = 0
                NMM = 2 * IC
                for ic in range(IC):
                    for hsrc in (h_hi, h_lo):
                        s.run("tensor", lambda t, ic=ic, db=db, hsrc=hsrc, pbank=pbank,
                              first=(nmm == 0), last=(nmm == NMM - 1): t.matmul(
                            pbank[0:128, 0:TF],
                            dn_bf[db][:, ic * 128:(ic + 1) * 128],
                            hsrc[:, ic * TF:(ic + 1) * TF],
                            start=first, stop=last),
                            inc="t_dn" if nmm == NMM - 1 else None)
                        nmm += 1
                s.dn_mm_done[(tf, hc)] = s.val("t_dn")
                s.wait("vector", "t_dn", s.dn_mm_done[(tf, hc)])
                s.run("vector", lambda v, hc=hc, pbank=pbank: v.tensor_scalar(
                    oo[:, hc * TF:(hc + 1) * TF], pbank[0:128, 0:TF],
                    ds_cols[:, hc:hc + 1], None, ALU.mult), inc="v_dn")
                s.dn_ev_done[hc] = s.val("v_dn")
                if hc >= 1:
                    _c_stat_chunk(oo, hc - 1, EC, ready=("v_dn", s.dn_ev_done[hc - 1]))
            _c_stat_chunk(oo, EC - 1, EC, ready=("v_dn", s.dn_ev_done[EC - 1]))
            v_dn_now = s.val("v_dn")
            s.wait("tensor", "v_dn", v_dn_now)
            s.wait("vector", "v_dn", v_dn_now)
            _c_stat_rows(E)
            s.run("vector", lambda v: v.tensor_tensor(
                oo[:, :].rearrange("p (ec t) -> p ec t", t=TF),
                oo[:, :].rearrange("p (ec t) -> p ec t", t=TF),
                bCR[:, :].rearrange("p (a t) -> p a t", a=1).broadcast_to((128, EC, TF)), ALU.mult), inc="v_ot")
            s.wait("vector", "v_ot")
            s.run("vector", lambda v: v.tensor_tensor(
                oo[:, :].rearrange("p (ec t) -> p ec t", t=TF),
                oo[:, :].rearrange("p (ec t) -> p ec t", t=TF),
                bCM[:, :].rearrange("p (a t) -> p a t", a=1).broadcast_to((128, EC, TF)), ALU.add), inc="v_ot")
            s.wait("vector", "v_ot")
            s.run("vector", lambda v: v.tensor_add(oo[:, :], x2[:, :], oo[:, :]), inc="v_ot")
            s.wait("vector", "v_ot")
            v_ot_now = s.val("v_ot")
            s.wait("sync", "v_ot", v_ot_now)
            s.run("sync", lambda e, tb=tb: e.dma_start(
                out=out_fm.rearrange("(ec p) t -> p ec t", p=128)[:, :, tb:tb + TF],
                in_=oo[:, :]), inc="d_out", n=16)
        s.wait("sync", "d_out")

    s.tile_mm_done = {}
    s.ts_base = 0
    s.xa_done = {}
    s.xa_tdone = {}
    s.rw_done = {}
    s.bc_done = {}
    s.vbc_done = {}
    s.sq_done = {}
    s.rope_done = {}
    s.pair_ld = {}
    s.unit_mm = {}
    s.gsp_done = {}
    s.bank_free = {}
    s.ep_free = None
    s.pb7_free = None
    s.sp_free = None
    s.o_free = None
    s.lnspill_done = {}
    s.wp_split_done = {}
    s.pj_mm_done = {}
    s.gu_mm_done = {}
    s.dn_mm_done = {}
    s.pj_ev_done = {}
    s.dn_ev_done = {}
    s.csq_done = {}
    s.fe_s_done = {}
    s.fe_v1_done = {}
    s.fe_v2_done = {}

    s.mark("ph0")
    phase0()
    s.barrier("ph0")
    s.mark("A0pre")
    phaseA0()
    s.barrier("phA0pre")
    for hp in range(cfg.HPC):
        s.mark(f"A{hp}")
        phaseA(hp)
        s.barrier(f"phA{hp}")
        s.mark(f"B{hp}")
        phaseB(hp)
        s.barrier(f"phB{hp}")
    s.mark("C")
    phaseC()

    block = nc.Block("main").__enter__()
    block.sync(lambda e: s.replay("sync", e))
    block.tensor(lambda e: s.replay("tensor", e))
    block.vector(lambda e: s.replay("vector", e))
    block.scalar(lambda e: s.replay("scalar", e))
    block.gpsimd(lambda e: s.replay("gpsimd", e))
    block.__exit__(None, None, None)
    return nc


def _rope_perm():
    return np.concatenate([np.arange(0, 128, 2), np.arange(1, 128, 2)])


def prepare_inputs(cfg, x, Wqkv, Wproj, gw, uw, dw, gs, us, ds):
    T, E, I, NC, D = cfg.T, cfg.E, cfg.I, cfg.NC, cfg.D
    QT = cfg.HD
    x = np.asarray(x, dtype=np.float32).reshape(T, E)
    Wqkv = np.asarray(Wqkv, dtype=np.float32)
    Wproj = np.asarray(Wproj, dtype=np.float32)
    gw = np.asarray(gw, dtype=np.float32)
    uw = np.asarray(uw, dtype=np.float32)
    dw = np.asarray(dw, dtype=np.float32)
    gs = np.asarray(gs, dtype=np.float32)
    us = np.asarray(us, dtype=np.float32)
    ds = np.asarray(ds, dtype=np.float32)

    x_fm = np.ascontiguousarray(x.T)
    HC = QT // 128
    permw = np.concatenate([np.arange(0, HC, 2), np.arange(1, HC, 2)])
    wprojT = np.ascontiguousarray(
        Wproj.T.reshape(HC, 128, -1)[permw].reshape(QT, -1))
    gateT = np.ascontiguousarray(gw.T)
    upT = np.ascontiguousarray(uw.T)
    downT = np.ascontiguousarray(dw.T)
    gate_s = np.ascontiguousarray(gs.reshape(cfg.IC, 128).T)
    up_s = np.ascontiguousarray(us.reshape(cfg.IC, 128).T)
    down_s = np.ascontiguousarray(ds.reshape(cfg.EC, 128).T)

    perm = _rope_perm()
    in_maps = []
    for c in range(NC):
        cols = []
        for hp in range(cfg.HPC):
            h = cfg.HPC * c + hp
            q = Wqkv[h * D:(h + 1) * D][perm]
            k = Wqkv[QT + h * D:QT + (h + 1) * D][perm]
            v = Wqkv[2 * QT + h * D:2 * QT + (h + 1) * D]
            cols.append(np.concatenate([q, k, v], axis=0))
        wq = np.concatenate(cols, axis=0)
        wqkvT = np.ascontiguousarray(wq.T)
        x_own = np.ascontiguousarray(x_fm[:, c * cfg.To:(c + 1) * cfg.To])
        in_maps.append({
            "x_fm": x_fm, "x_own": x_own, "wqkvT": wqkvT, "wprojT": wprojT,
            "gateT": gateT, "upT": upT, "downT": downT,
            "gate_s": gate_s, "up_s": up_s, "down_s": down_s,
        })
    return in_maps


def kernel(L0_attn_x, L0_attn_W_qkv, L0_attn_W_proj, L0_ffn_gate_w,
           L0_ffn_up_w, L0_ffn_down_w, L0_ffn_gate_s, L0_ffn_up_s,
           L0_ffn_down_s):
    cfg = Cfg()
    in_maps = prepare_inputs(cfg, L0_attn_x, L0_attn_W_qkv, L0_attn_W_proj,
                             L0_ffn_gate_w, L0_ffn_up_w, L0_ffn_down_w,
                             L0_ffn_gate_s, L0_ffn_up_s, L0_ffn_down_s)
    nc = build_program(cfg)
    res = run_bass_kernel_spmd(nc, in_maps, core_ids=list(range(cfg.NC)))
    parts = [res.results[c]["out_fm"] for c in range(cfg.NC)]
    out = np.concatenate(parts, axis=1)
    return np.ascontiguousarray(out.T).reshape(1, cfg.T, cfg.E)


# version 20 (final)
# speedup vs baseline: 1.1094x; 1.1094x over previous
import math
import numpy as np

import concourse.bass as bass
import concourse.mybir as mybir
from concourse.bass_utils import run_bass_kernel_spmd

F32 = mybir.dt.float32
BF16 = mybir.dt.bfloat16
AF = mybir.ActivationFunctionType
ALU = mybir.AluOpType

LN_EPS = 1e-5
EXP_SHIFT = -12.0
INV_SQRT_D = 1.0 / math.sqrt(128.0)
NEG_LN1E4_64 = -math.log(10000.0) / 64.0

ENGS = ("sync", "tensor", "vector", "scalar", "gpsimd")


class Cfg:
    def __init__(self, T=8192, E=2048, I=5632, NC=8):
        self.T, self.E, self.I, self.NC = T, E, I, NC
        self.D = 128
        self.HPC = 2
        self.HD = self.NC * self.HPC * self.D
        self.HC = self.HD // 128
        self.To = T // NC
        self.TTA = min(256, T)
        self.TQ = min(512, T)
        self.TF = min(512, self.To)
        self.EC = E // 128
        self.IC = I // 128
        assert T % self.TTA == 0 and T % self.TQ == 0 and self.To % self.TF == 0
        assert self.TQ % 128 == 0 and E % 128 == 0 and I % 128 == 0


class Sched:
    def __init__(self, nc):
        self.nc = nc
        self.q = {k: [] for k in ENGS}
        self._sems = {}
        self.cnt = {}
        self.marks = []

    def sem(self, name):
        if name not in self._sems:
            self._sems[name] = self.nc.alloc_semaphore(name)
            self.cnt[name] = 0
        return self._sems[name]

    def val(self, name):
        self.sem(name)
        return self.cnt[name]

    def wait(self, eng, name, val=None):
        h = self.sem(name)
        v = self.cnt[name] if val is None else val
        if v <= 0:
            return
        self.q[eng].append(lambda e, h=h, v=v: e.wait_ge(h, v))

    def run(self, eng, fn, inc=None, n=1):
        h = None
        if inc is not None:
            h = self.sem(inc)
            self.cnt[inc] += n
        def g(e, fn=fn, h=h, n=n):
            ins = fn(e)
            if h is not None:
                ins.then_inc(h, n)
        self.q[eng].append(g)

    def mark(self, name):
        self.marks.append((name, {k: len(self.q[k]) for k in ENGS}))

    def replay(self, eng_name, engine):
        lst = self.q[eng_name]
        bounds = [(nm, idx[eng_name]) for nm, idx in self.marks] + [("end", len(lst))]
        pos = 0
        for i, (nm, start) in enumerate(bounds[:-1]):
            end = bounds[i + 1][1]
            if end > start:
                with self.nc.named_scope(nm):
                    for f in lst[start:end]:
                        f(engine)
            pos = end

    def barrier(self, name):
        h = self.sem(name)
        target = self.cnt[name] + len(ENGS)
        for eng in ENGS:
            self.q[eng].append(lambda e, h=h: e.sem_inc(h, 1))
        self.cnt[name] = target
        for eng in ENGS:
            self.q[eng].append(lambda e, h=h, t=target: e.wait_ge(h, t))


def _r32(n):
    return (n + 31) // 32 * 32


def build_program(cfg: Cfg):
    nc = bass.Bass("TRN2", target_bir_lowering=False)
    T, E, I = cfg.T, cfg.E, cfg.I
    EC, IC, HC = cfg.EC, cfg.IC, cfg.HC
    TTA, TQ, TF, To = cfg.TTA, cfg.TQ, cfg.TF, cfg.To
    NTA, NQ, NF = T // TTA, T // TQ, To // TF
    KPT = TTA // 128
    W3 = 3 * cfg.D

    x_fm = nc.dram_tensor("x_fm", [E, T], F32, kind="ExternalInput").ap()
    x_own = nc.dram_tensor("x_own", [E, To], F32, kind="ExternalInput").ap()
    wqkvT = nc.dram_tensor("wqkvT", [E, cfg.HPC * W3], F32, kind="ExternalInput").ap()
    wprojT = nc.dram_tensor("wprojT", [cfg.HD, E], F32, kind="ExternalInput").ap()
    gateT = nc.dram_tensor("gateT", [E, I], F32, kind="ExternalInput").ap()
    upT = nc.dram_tensor("upT", [E, I], F32, kind="ExternalInput").ap()
    downT = nc.dram_tensor("downT", [I, E], F32, kind="ExternalInput").ap()
    gate_s = nc.dram_tensor("gate_s", [128, IC], F32, kind="ExternalInput").ap()
    up_s = nc.dram_tensor("up_s", [128, IC], F32, kind="ExternalInput").ap()
    down_s = nc.dram_tensor("down_s", [128, EC], F32, kind="ExternalInput").ap()
    out_fm = nc.dram_tensor("out_fm", [E, To], F32, kind="ExternalOutput").ap()

    ct_d = nc.dram_tensor("ct_d", [128, T], F32).ap()
    st_d = nc.dram_tensor("st_d", [128, T], F32).ap()
    ln1_hi_d = nc.dram_tensor("ln1_hi_d", [128, EC * T], BF16).ap()
    ln1_lo_d = nc.dram_tensor("ln1_lo_d", [128, EC * T], BF16).ap()
    rows_d = nc.dram_tensor("rows_d", [1, 8 * TTA], F32).ap()
    qsp_hi = nc.dram_tensor("qsp_hi", [128, T], BF16).ap()
    qsp_lo = nc.dram_tensor("qsp_lo", [128, T], BF16).ap()
    agin = [nc.dram_tensor(f"agin{h}", [128, T], F32) for h in range(cfg.HPC)]
    agout = [nc.dram_tensor(f"agout{h}", [cfg.NC * 128, T], F32) for h in range(cfg.HPC)]

    SBUF_BYTES = 212800
    base0, _ = nc.bump_sbuf(SBUF_BYTES, side="left", align_bytes=32)
    cur = [0]

    def at(name, shape, dtype, o=None):
        if o is None:
            o = cur[0]
            cur[0] = o + _r32(int(np.prod(shape[1:])) * mybir.dt.size(dtype))
        return nc.alloc_sbuf_tensor_at(name, list(shape), dtype, offset=base0 + o)

    ones_col_bf = at("ones_col_bf", [128, 2], BF16)
    ones_row_f32 = at("ones_row_f32", [1, 128], F32)
    ones_col_f32 = at("ones_col_f32", [128, 1], F32)
    swapmat = at("swapmat", [128, 128], F32)
    ident = at("ident", [128, 128], F32)
    invf_col = at("invf_col", [128, 1], F32)
    pih_col = at("pih_col", [128, 1], F32)
    sgn_col = at("sgn_col", [128, 1], F32)
    shift_col = at("shift_col", [128, 1], F32)
    tmp_col = at("tmp_col", [128, 4], F32)
    gs_cols = at("gs_cols", [128, IC], F32)
    us_cols = at("us_cols", [128, IC], F32)
    ds_cols = at("ds_cols", [128, EC], F32)
    rows = at("rows", [1, 4 * 512], F32)
    PBASE = cur[0]
    assert PBASE < 16 * 1024, PBASE

    cur[0] = PBASE
    k_hi = at("k_hi", [128, T], BF16)
    k_lo = at("k_lo", [128, T], BF16)
    v_hi = at("v_hi", [128, T], BF16)
    v_lo = at("v_lo", [128, T], BF16)
    ASTREAM = cur[0]
    XA0OFF = cur[0]
    xa0 = at("xa", [128, EC * TTA], F32)
    lhA = at("lhA", [128, EC * 2 * TTA], BF16)
    llA = at("llA", [128, EC * 2 * TTA], BF16)
    lh = [lhA[:, 0:EC * TTA], lhA[:, EC * TTA:]]
    ll = [llA[:, 0:EC * TTA], llA[:, EC * TTA:]]
    SCOFF = cur[0]
    sc_f32 = at("sc_f32", [128, EC * TTA], F32)
    XA1OFF = cur[0]
    xa1 = at("xa1", [128, EC * TTA], F32)
    wq_hi = at("wq_hi", [128, EC * W3], BF16)
    wq_lo = at("wq_lo", [128, EC * W3], BF16)
    PAIR = 2 * TTA
    ACL = cur[0]
    ep_f32 = at("ep_f32", [128, PAIR], F32)
    rp_t1 = at("rp_t1", [128, PAIR], F32)
    rp_t2 = at("rp_t2", [128, PAIR], F32)
    cttile = [at(f"cttile{b}", [128, PAIR], F32) for b in range(2)]
    sttile = [at(f"sttile{b}", [128, PAIR], F32) for b in range(2)]
    sp_hi = at("sp_hi", [128, PAIR], BF16)
    sp_lo = at("sp_lo", [128, PAIR], BF16)
    ACL_BRBM = cur[0]
    cur[0] = ACL_BRBM + 2 * _r32(2 * TTA * 4)
    xh = at("xh", [128, EC * TTA], BF16, ACL)
    xl = at("xl", [128, EC * TTA], BF16, ACL + _r32(EC * TTA * 2))
    sqb = [at("sqb0", [128, EC * TTA], BF16, SCOFF),
           at("sqb1", [128, EC * TTA], BF16, SCOFF + _r32(EC * TTA * 2))]
    assert 2 * _r32(EC * TTA * 4) <= 4 * _r32(T * 2)
    xa2 = at("xa2", [128, EC * TTA], F32, PBASE)
    xa3 = at("xa3", [128, EC * TTA], F32, PBASE + _r32(EC * TTA * 4))
    if 3 * _r32(EC * TTA * 4) <= 4 * _r32(T * 2):
        xa4 = at("xa4", [128, EC * TTA], F32, PBASE + 2 * _r32(EC * TTA * 4))
    else:
        xa4 = at("xa4", [128, EC * TTA], F32)
    _brbm = 2 * _r32(6 * TTA * 4)
    if 3 * _r32(EC * TTA * 4) + _brbm <= 4 * _r32(T * 2):
        _bro = PBASE + 3 * _r32(EC * TTA * 4)
        bR = at("bR", [128, 6 * TTA], F32, _bro)
        bM = at("bM", [128, 6 * TTA], F32, _bro + _r32(6 * TTA * 4))
    else:
        bR = at("bR", [128, 6 * TTA], F32)
        bM = at("bM", [128, 6 * TTA], F32)
    pbh1 = at("pbh1", [128, EC * PAIR], BF16, XA0OFF)
    pbl1 = at("pbl1", [128, EC * PAIR], BF16, XA1OFF)
    pbuf_hi = [lhA, pbh1]
    pbuf_lo = [llA, pbl1]
    AB_END = cur[0]
    cur[0] = ASTREAM
    q_hi = at("q_hi", [128, TQ], BF16)
    q_lo = at("q_lo", [128, TQ], BF16)
    p_sb = [at(f"p_sb{b}", [128, TQ], F32) for b in range(3)]
    p_hi = [at(f"p_hi{b}", [128, TQ], BF16) for b in range(3)]
    p_lo = [at(f"p_lo{b}", [128, TQ], BF16) for b in range(3)]
    o_sb = at("o_sb", [128, TQ], F32)
    rb_sb = at("rb_sb", [128, TQ], F32)
    msk = at("msk", [128, (TQ // 128) * TQ], F32)
    B_END = cur[0]
    assert B_END <= AB_END

    R0 = PBASE
    R1 = R0 + _r32(HC * TF * 4)
    R2 = R1 + _r32(max(HC, EC) * TF * 4)
    R3 = R2 + _r32(EC * TF * 4)
    R3SZ = max(2 * _r32(HC * 128 * 4) + 4 * _r32(HC * 128 * 2),
               0)
    HBYTES = 2 * _r32(IC * TF * 2)
    R5 = R3 + max(R3SZ, HBYTES)
    y_f32 = at("y_f32", [128, HC * TF], F32, R0)
    ao = at("ao", [128, EC * TF], F32, R0)
    gu_bf = [at(f"gu_bf{b}", [128, 2 * EC * 128], BF16,
                R0 + b * _r32(2 * EC * 128 * 2)) for b in range(2)]
    dn_bf = [at(f"dn_bf{b}", [128, IC * 128], BF16,
                R0 + b * _r32(IC * 128 * 2)) for b in range(2)]
    y_hi = at("y_hi", [128, HC * TF], BF16, R1)
    y_lo = at("y_lo", [128, HC * TF], BF16, R1 + _r32(HC * TF * 2))
    x2h = at("x2h", [128, EC * TF], BF16, R1)
    x2l = at("x2l", [128, EC * TF], BF16, R1 + _r32(EC * TF * 2))
    oo = at("oo", [128, EC * TF], F32, R1)
    x2 = at("x2", [128, EC * TF], F32, R2)
    wp_f32 = [at(f"wp_f32{b}", [128, HC * 128], F32,
                 R3 + b * _r32(HC * 128 * 4)) for b in range(2)]
    _wpb = R3 + 2 * _r32(HC * 128 * 4)
    wp_hi = [at(f"wp_hi{b}", [128, HC * 128], BF16,
                _wpb + b * _r32(HC * 128 * 2)) for b in range(2)]
    wp_lo = [at(f"wp_lo{b}", [128, HC * 128], BF16,
                _wpb + (2 + b) * _r32(HC * 128 * 2)) for b in range(2)]
    h_hi = at("h_hi", [128, IC * TF], BF16, R3)
    h_lo = at("h_lo", [128, IC * TF], BF16, R3 + _r32(IC * TF * 2))
    cs1 = at("cs1", [128, TF], F32, R5)
    cs2 = at("cs2", [128, TF], F32, R5 + _r32(TF * 4))
    bCR = at("bCR", [128, TF], F32, R5 + 2 * _r32(TF * 4))
    bCM = at("bCM", [128, TF], F32, R5 + 3 * _r32(TF * 4))
    csA = [cs1, at("cs3", [128, TF], F32, R5 + 4 * _r32(TF * 4))]
    csB = [cs2, at("cs4", [128, TF], F32, R5 + 5 * _r32(TF * 4))]
    C_END = R5 + 6 * _r32(TF * 4)
    assert max(AB_END, C_END) <= SBUF_BYTES, (AB_END, C_END, SBUF_BYTES)

    ct_full = at("ct_full", [128, T], F32, PBASE)
    st_full = at("st_full", [128, T], F32, PBASE + _r32(T * 4))
    rb_t1 = at("rb_t1", [128, T], F32, PBASE + 2 * _r32(T * 4))
    rb_t2 = at("rb_t2", [128, T], F32, PBASE + 3 * _r32(T * 4))

    pb = [nc.alloc_psum_tensor(f"pb{i}", [128, 512], F32) for i in range(8)]

    s = Sched(nc)

    def row(i, n):
        return rows[0:1, i * 512:i * 512 + n]

    def phase0():
        s.run("gpsimd", lambda g: g.memset(ones_col_bf[:, :], 1.0), inc="g0")
        s.run("gpsimd", lambda g: g.memset(ones_row_f32[:, :], 1.0), inc="g0")
        s.run("gpsimd", lambda g: g.memset(ones_col_f32[:, :], 1.0), inc="g0")
        s.run("gpsimd", lambda g: g.memset(ident[:, :], 1.0), inc="g0")
        s.run("gpsimd", lambda g: g.memset(swapmat[:, :], 1.0), inc="g0")
        s.run("gpsimd", lambda g: g.memset(sgn_col[:, :], 1.0), inc="g0")
        s.run("gpsimd", lambda g: g.memset(shift_col[:, :], EXP_SHIFT), inc="g0")
        s.run("gpsimd", lambda g: g.memset(ct_full[:, 0:1], 1.0), inc="g0")
        s.run("gpsimd", lambda g: g.memset(st_full[:, 0:1], 0.0), inc="g0")
        s.wait("gpsimd", "g0")
        s.run("gpsimd", lambda g: g.affine_select(
            ident[:, :], ident[:, :], pattern=[[1, 128]],
            compare_op=ALU.is_equal, fill=0.0, base=0, channel_multiplier=-1), inc="g0")
        s.run("gpsimd", lambda g: g.affine_select(
            swapmat[0:64, :], swapmat[0:64, :], pattern=[[1, 128]],
            compare_op=ALU.is_equal, fill=0.0, base=-64, channel_multiplier=-1), inc="g0")
        s.wait("gpsimd", "g0")
        s.run("gpsimd", lambda g: g.affine_select(
            swapmat[64:128, :], swapmat[64:128, :], pattern=[[1, 128]],
            compare_op=ALU.is_equal, fill=0.0, base=0, channel_multiplier=-1), inc="g0")
        s.run("gpsimd", lambda g: g.affine_select(
            sgn_col[:, :], sgn_col[:, :], pattern=[[0, 1]],
            compare_op=ALU.is_ge, fill=-1.0, base=-64, channel_multiplier=1), inc="g0")
        s.run("gpsimd", lambda g: g.iota(
            invf_col[0:64, :], pattern=[[0, 1]], base=0, channel_multiplier=1,
            allow_small_or_imprecise_dtypes=True), inc="g0")
        s.run("gpsimd", lambda g: g.iota(
            invf_col[64:128, :], pattern=[[0, 1]], base=0, channel_multiplier=1,
            allow_small_or_imprecise_dtypes=True), inc="g0")
        s.wait("scalar", "g0")
        s.run("scalar", lambda sc: sc.activation(
            invf_col[:, :], invf_col[:, :], AF.Exp, bias=0.0,
            scale=NEG_LN1E4_64), inc="s0")
        s.run("scalar", lambda sc: sc.activation(
            pih_col[:, :], ones_col_f32[:, :], AF.Arctan, bias=0.0, scale=1.0), inc="s0")
        s.wait("vector", "s0")
        s.run("vector", lambda v: v.tensor_scalar_mul(pih_col[:, :], pih_col[:, :], 2.0), inc="v0")
        s.wait("scalar", "v0")
        s.run("scalar", lambda sc: sc.activation(
            ct_full[:, 1:2], invf_col[:, :], AF.Sin, bias=pih_col[:, :], scale=-1.0), inc="s0")
        s.run("scalar", lambda sc: sc.activation(
            tmp_col[:, 0:1], invf_col[:, :], AF.Sin, bias=0.0, scale=1.0), inc="s0")
        s.wait("vector", "s0")
        s.run("vector", lambda v: v.tensor_mul(
            st_full[:, 1:2], tmp_col[:, 0:1], sgn_col[:, :]), inc="v0")
        Tc = 2
        while Tc <= T // 2:
            s.wait("vector", "v0")
            s.run("vector", lambda v, Tc=Tc: v.tensor_mul(
                tmp_col[:, 1:2], ct_full[:, Tc - 1:Tc], ct_full[:, 1:2]), inc="v0")
            s.run("vector", lambda v, Tc=Tc: v.tensor_mul(
                tmp_col[:, 2:3], st_full[:, Tc - 1:Tc], st_full[:, 1:2]), inc="v0")
            s.run("vector", lambda v, Tc=Tc: v.tensor_mul(
                tmp_col[:, 3:4], st_full[:, Tc - 1:Tc], ct_full[:, 1:2]), inc="v0")
            s.run("vector", lambda v, Tc=Tc: v.tensor_mul(
                tmp_col[:, 0:1], ct_full[:, Tc - 1:Tc], st_full[:, 1:2]), inc="v0")
            s.wait("vector", "v0")
            s.run("vector", lambda v, Tc=Tc: v.tensor_sub(
                ct_full[:, Tc:Tc + 1], tmp_col[:, 1:2], tmp_col[:, 2:3]), inc="v0")
            s.run("vector", lambda v, Tc=Tc: v.tensor_add(
                st_full[:, Tc:Tc + 1], tmp_col[:, 3:4], tmp_col[:, 0:1]), inc="v0")
            s.wait("vector", "v0")
            Lc = min(Tc, T - Tc)
            if Lc > 1:
                s.run("vector", lambda v, Tc=Tc, Lc=Lc: v.tensor_scalar(
                    rb_t1[:, 1:Lc], st_full[:, 1:Lc], st_full[:, Tc:Tc + 1],
                    None, ALU.mult), inc="v0")
                s.run("vector", lambda v, Tc=Tc, Lc=Lc: v.tensor_scalar(
                    rb_t2[:, 1:Lc], ct_full[:, 1:Lc], st_full[:, Tc:Tc + 1],
                    None, ALU.mult), inc="v0")
                s.wait("vector", "v0")
                s.run("vector", lambda v, Tc=Tc, Lc=Lc: v.scalar_tensor_tensor(
                    ct_full[:, Tc + 1:Tc + Lc], ct_full[:, 1:Lc],
                    ct_full[:, Tc:Tc + 1], rb_t1[:, 1:Lc],
                    ALU.mult, ALU.subtract), inc="v0")
                s.run("vector", lambda v, Tc=Tc, Lc=Lc: v.scalar_tensor_tensor(
                    st_full[:, Tc + 1:Tc + Lc], st_full[:, 1:Lc],
                    ct_full[:, Tc:Tc + 1], rb_t2[:, 1:Lc],
                    ALU.mult, ALU.add), inc="v0")
                s.wait("vector", "v0")
            Tc *= 2
        s.run("vector", lambda v: v.engine_nop(), inc="v0")
        s.wait("sync", "v0")
        s.run("sync", lambda e: e.dma_start(out=ct_d, in_=ct_full[:, :]), inc="dsy", n=16)
        s.run("sync", lambda e: e.dma_start(out=st_d, in_=st_full[:, :]), inc="dsy", n=16)
        s.run("sync", lambda e: e.dma_start(out=gs_cols[:, :], in_=gate_s), inc="dsy", n=16)
        s.run("sync", lambda e: e.dma_start(out=us_cols[:, :], in_=up_s), inc="dsy", n=16)
        s.run("sync", lambda e: e.dma_start(out=ds_cols[:, :], in_=down_s), inc="dsy", n=16)
        s.wait("sync", "dsy")

    def phaseA0():
        xab = [xa0, xa1, xa2, xa3, xa4]
        NSL = len(xab)
        xhab = [xh, xl]
        stbanks = [(pb[4], pb[5]), (pb[2], pb[3])]
        assert 2 * TTA <= 512
        def rowh(i, half):
            return rows[0:1, i * 512 + half * TTA: i * 512 + half * TTA + TTA]

        def load(tt):
            sl = tt % NSL
            if tt >= NSL:
                s.wait("sync", "v_ln", s.xa_done.get(tt - NSL, 0))
            s.run("sync", lambda e, t0=tt * TTA, xa=xab[sl]: e.dma_start(
                out=xa[:, :],
                in_=x_fm.rearrange("(ec p) t -> p ec t", p=128)[:, :, t0:t0 + TTA]),
                inc=f"d_x{sl}", n=16)
            s.xa_ld[tt] = s.val(f"d_x{sl}")

        def casts(tt):
            sl, b = tt % NSL, tt % 2
            xa, xhb = xab[sl], xhab[b]
            s.wait("scalar", f"d_x{sl}", s.xa_ld[tt])
            if tt >= 2:
                s.wait("scalar", "t_st", s.xa_tdone.get(tt - 2, 0))
                s.wait("scalar", "t_sq2", s.sq_done.get(tt - 2, 0))
            s.run("scalar", lambda sc, xa=xa, xhb=xhb: sc.activation(
                xhb[:, :], xa[:, :], AF.Copy, bias=0.0, scale=1.0), inc="s_xc")
            s.run("scalar", lambda sc, xa=xa, b=b: sc.activation(
                sqb[b][:, :], xa[:, :], AF.Square, bias=0.0, scale=1.0), inc="s_xc")
            s.sxc_done[tt] = s.val("s_xc")

        def sums(tt):
            b = tt % 2
            xhb = xhab[b]
            sums_, sumsq = stbanks[b]
            s.wait("tensor", "s_xc", s.sxc_done[tt])
            if tt >= 2:
                s.wait("tensor", "v_rw", s.rwA_done.get(tt - 2, 0))
            for ec in range(EC):
                s.run("tensor", lambda t, ec=ec, sums_=sums_, xhb=xhb,
                      first=(ec == 0), last=(ec == EC - 1): t.matmul(
                    sums_[0:1, 0:TTA], ones_col_bf[:, 0:1],
                    xhb[:, ec * TTA:(ec + 1) * TTA], start=first, stop=last),
                    inc="t_st" if ec == EC - 1 else None)
            s.xa_tdone[tt] = s.val("t_st")
            for ec in range(EC):
                s.run("tensor", lambda t, ec=ec, sumsq=sumsq, b=b,
                      first=(ec == 0), last=(ec == EC - 1): t.matmul(
                    sumsq[0:1, 0:TTA], ones_col_bf[:, 0:1],
                    sqb[b][:, ec * TTA:(ec + 1) * TTA], start=first, stop=last),
                    inc="t_sq2" if ec == EC - 1 else None)
            s.sq_done[tt] = s.val("t_sq2")

        def rowsA(tt):
            b = tt % 2
            sums_, sumsq = stbanks[b]
            s.wait("vector", "t_st", s.xa_tdone[tt])
            s.wait("vector", "t_sq2", s.sq_done[tt])
            if tt >= 2:
                s.wait("vector", f"d_rs{(tt - 2) % 4}", s.rs_done.get(tt - 2, 0))
            s.run("vector", lambda v, b=b, sums_=sums_: v.tensor_scalar_mul(
                rowh(0, b), sums_[0:1, 0:TTA], -1.0 / E), inc="v_rw")
            s.run("vector", lambda v, b=b, sumsq=sumsq: v.tensor_scalar_mul(
                rowh(1, b), sumsq[0:1, 0:TTA], 1.0 / E), inc="v_rw")
            s.wait("vector", "v_rw")
            s.run("vector", lambda v, b=b: v.tensor_mul(rowh(2, b), rowh(0, b), rowh(0, b)), inc="v_rw")
            s.wait("vector", "v_rw")
            s.run("vector", lambda v, b=b: v.tensor_sub(rowh(3, b), rowh(1, b), rowh(2, b)), inc="v_rw")
            s.wait("vector", "v_rw")
            s.run("vector", lambda v, b=b: v.tensor_scalar_add(rowh(3, b), rowh(3, b), LN_EPS), inc="v_rw")
            s.wait("vector", "v_rw")
            s.run("vector", lambda v, b=b: v.reciprocal(rowh(2, b), rowh(3, b)), inc="v_rw")
            s.rwA_done[tt] = s.val("v_rw")

        def sqrt_(tt):
            b = tt % 2
            s.wait("scalar", "v_rw", s.rwA_done[tt])
            if tt >= 2:
                s.wait("scalar", f"d_rs{(tt - 2) % 4}", s.rs_done.get(tt - 2, 0))
            s.run("scalar", lambda sc, b=b: sc.activation(
                rowh(2, b), rowh(2, b), AF.Sqrt, bias=0.0, scale=1.0), inc="s_rw")
            s.srw_done[tt] = s.val("s_rw")

        def rowsB(tt):
            b = tt % 2
            s.wait("vector", "s_rw", s.srw_done[tt])
            s.run("vector", lambda v, b=b: v.tensor_mul(
                rowh(3, b), rowh(0, b), rowh(2, b)), inc="v_rw")
            s.rwB_done[tt] = s.val("v_rw")

        def spill_rows(tt):
            b = tt % 2
            r4 = tt % 4
            r0 = r4 * 2 * TTA
            s.wait("scalar", "v_rw", s.rwB_done[tt])
            if tt >= 4:
                s.wait("scalar", f"d_bc{r4}", s.bcld_done.get(tt - 4, 0))
            s.run("scalar", lambda e, b=b, r0=r0: e.dma_start(
                out=rows_d[0:1, r0:r0 + TTA], in_=rowh(2, b)),
                inc=f"d_rs{r4}", n=16)
            s.run("scalar", lambda e, b=b, r0=r0: e.dma_start(
                out=rows_d[0:1, r0 + TTA:r0 + 2 * TTA], in_=rowh(3, b)),
                inc=f"d_rs{r4}", n=16)
            s.rs_done[tt] = s.val(f"d_rs{r4}")

        def bcast_ld(tt):
            b = tt % 2
            q = tt % 6
            r0 = (tt % 4) * 2 * TTA
            r4 = tt % 4
            s.wait("sync", f"d_rs{r4}", s.rs_done[tt])
            if tt >= 6:
                s.wait("sync", "v_ln", s.xa_done.get(tt - 6, 0))
            s.run("sync", lambda e, q=q, r0=r0: e.dma_start(
                out=bR[:, q * TTA:(q + 1) * TTA],
                in_=rows_d[0:1, r0:r0 + TTA].broadcast_to((128, TTA))),
                inc=f"d_bc{r4}", n=16)
            s.run("sync", lambda e, q=q, r0=r0: e.dma_start(
                out=bM[:, q * TTA:(q + 1) * TTA],
                in_=rows_d[0:1, r0 + TTA:r0 + 2 * TTA].broadcast_to((128, TTA))),
                inc=f"d_bc{r4}", n=16)
            s.bcld_done[tt] = s.val(f"d_bc{r4}")

        def tail(tt):
            sl, b = tt % NSL, tt % 2
            q = tt % 6
            t0 = tt * TTA
            xa = xab[sl]
            bRh = bR[:, q * TTA:(q + 1) * TTA]
            bMh = bM[:, q * TTA:(q + 1) * TTA]
            s.wait("vector", f"d_bc{tt % 4}", s.bcld_done[tt])
            if tt == 0:
                s.run("vector", lambda v, xa=xa: v.tensor_scalar_add(
                    lh[1][:, 0:TTA], xa[:, 0:TTA], 0.0), inc="v_ln")
                s.wait("vector", "v_ln")
            s.run("vector", lambda v, xa=xa, bRh=bRh: v.tensor_tensor(
                xa[:, :].rearrange("p (ec t) -> p ec t", t=TTA),
                xa[:, :].rearrange("p (ec t) -> p ec t", t=TTA),
                bRh.rearrange("p (a t) -> p a t", a=1).broadcast_to((128, EC, TTA)), ALU.mult), inc="v_ln")
            s.wait("vector", "v_ln")
            s.run("vector", lambda v, xa=xa, bMh=bMh: v.tensor_tensor(
                xa[:, :].rearrange("p (ec t) -> p ec t", t=TTA),
                xa[:, :].rearrange("p (ec t) -> p ec t", t=TTA),
                bMh.rearrange("p (a t) -> p a t", a=1).broadcast_to((128, EC, TTA)), ALU.add), inc="v_ln")
            s.wait("vector", "v_ln")
            if tt >= 2:
                s.wait("vector", f"d_ls{b}", s.ls_done.get(tt - 2, 0))
            s.run("vector", lambda v, b=b, xa=xa: v.tensor_scalar_add(
                lh[b][:, :], xa[:, :], 0.0), inc="v_ln")
            s.wait("vector", "v_ln")
            s.run("vector", lambda v, b=b, xa=xa: v.tensor_tensor(
                ll[b][:, :], xa[:, :], lh[b][:, :], ALU.subtract), inc="v_ln")
            s.xa_done[tt] = s.val("v_ln")

        def spill(tt):
            b = tt % 2
            W = EC * TTA
            s.wait("scalar", "v_ln", s.xa_done[tt])
            s.run("scalar", lambda e, b=b, tt=tt, W=W: e.dma_start(
                out=ln1_hi_d[:, tt * W:(tt + 1) * W],
                in_=lh[b][:, :]), inc=f"d_ls{b}", n=16)
            s.run("scalar", lambda e, b=b, tt=tt, W=W: e.dma_start(
                out=ln1_lo_d[:, tt * W:(tt + 1) * W],
                in_=ll[b][:, :]), inc=f"d_ls{b}", n=16)
            s.ls_done[tt] = s.val(f"d_ls{b}")

        for st in range(NTA + 5):
            if st >= 5:
                spill(st - 5)
            if st < NTA:
                load(st)
            if 2 <= st <= NTA + 1:
                rowsA(st - 2)
                sqrt_(st - 2)
            if 1 <= st <= NTA:
                casts(st - 1)
                sums(st - 1)
            if 2 <= st <= NTA + 1:
                rowsB(st - 2)
                spill_rows(st - 2)
                bcast_ld(st - 2)
            if 4 <= st <= NTA + 3:
                tail(st - 4)
        for nm in ("d_x0", "d_x1", "d_x2", "d_x3", "d_x4", "dsy",
                   "d_bc0", "d_bc1", "d_bc2", "d_bc3"):
            s.wait("sync", nm)
        s.wait("sync", "t_st", s.val("t_st"))
        s.wait("sync", "t_sq2", s.val("t_sq2"))
        for nm in ("d_ls0", "d_ls1", "d_rs0", "d_rs1", "d_rs2", "d_rs3"):
            s.wait("scalar", nm)

    def phaseA(hp):
        HEC = max(1, EC // 2)
        for half in range((EC + HEC - 1) // HEC):
            e0, e1 = half * HEC, min((half + 1) * HEC, EC)
            ne = e1 - e0
            s.wait("sync", "v_wq", s.val("v_wq"))
            s.run("sync", lambda e, hp=hp, e0=e0, e1=e1, ne=ne: e.dma_start(
                out=sc_f32[:, 0:ne * W3],
                in_=wqkvT.rearrange("(ec p) f -> p ec f", p=128)[
                    :, e0:e1, hp * W3:(hp + 1) * W3]),
                inc="d_wq", n=16)
            s.wait("vector", "d_wq")
            s.wait("vector", "t_qkv", s.val("t_qkv"))
            s.run("vector", lambda v, e0=e0, ne=ne: v.tensor_copy(
                wq_hi[:, e0 * W3:(e0 + ne) * W3], sc_f32[:, 0:ne * W3]), inc="v_wq")
            s.wait("vector", "v_wq")
            s.run("vector", lambda v, e0=e0, ne=ne: v.tensor_tensor(
                wq_lo[:, e0 * W3:(e0 + ne) * W3], sc_f32[:, 0:ne * W3],
                wq_hi[:, e0 * W3:(e0 + ne) * W3], ALU.subtract), inc="v_wq")

        assert NTA % 2 == 0
        NP = NTA // 2
        KPP = PAIR // 128
        qbanks = [pb[0], pb[1], pb[4], pb[5]]

        def emit_mm(u):
            pp, fc = u
            b = pp % 2
            t0 = pp * PAIR
            bh, bl = pbuf_hi[b], pbuf_lo[b]
            if fc == 0:
                if pp >= 2:
                    s.wait("sync", "t_qkv", s.tile_mm_done[(hp, pp - 2)])
                    s.wait("sync", "v_ep", s.rope_done.get((hp, pp - 2), 0))
                W = EC * TTA
                for n_ in range(2):
                    s.run("sync", lambda e, bh=bh, pp=pp, n_=n_, W=W: e.dma_start(
                        out=bh[:, n_ * W:(n_ + 1) * W],
                        in_=ln1_hi_d[:, (2 * pp + n_) * W:(2 * pp + n_ + 1) * W]),
                        inc=f"d_x{b}", n=16)
                    s.run("sync", lambda e, bl=bl, pp=pp, n_=n_, W=W: e.dma_start(
                        out=bl[:, n_ * W:(n_ + 1) * W],
                        in_=ln1_lo_d[:, (2 * pp + n_) * W:(2 * pp + n_ + 1) * W]),
                        inc=f"d_x{b}", n=16)
                s.run("sync", lambda e, b=b, t0=t0: e.dma_start(
                    out=cttile[b][:, :], in_=ct_d[:, t0:t0 + PAIR]), inc=f"d_x{b}", n=16)
                s.run("sync", lambda e, b=b, t0=t0: e.dma_start(
                    out=sttile[b][:, :], in_=st_d[:, t0:t0 + PAIR]), inc=f"d_x{b}", n=16)
                s.pair_ld[pp] = s.val(f"d_x{b}")
            pbank = qbanks[(3 * pp + fc) % 4]
            s.wait("tensor", f"d_x{b}", s.pair_ld[pp])
            if fc == 0:
                s.wait("tensor", "v_wq")
            key = ("ab", (3 * pp + fc) % 4)
            if key in s.bank_free:
                s.wait("tensor", s.bank_free[key][0], s.bank_free[key][1])
            nmm = 0
            NMM = 3 * EC
            for ec in range(EC):
                for (wsrc, xsrc) in ((wq_hi, bh), (wq_hi, bl), (wq_lo, bh)):
                    s.run("tensor", lambda t, wsrc=wsrc, xsrc=xsrc, ec=ec, fc=fc,
                          pbank=pbank, first=(nmm == 0), last=(nmm == NMM - 1): t.matmul(
                        pbank[0:128, 0:PAIR],
                        wsrc[:, ec * W3 + fc * 128:ec * W3 + fc * 128 + 128],
                        xsrc[:, :].rearrange("p (n ec t) -> p ec n t",
                                             ec=EC, t=TTA)[:, ec],
                        start=first, stop=last),
                        inc="t_qkv" if nmm == NMM - 1 else None)
                    nmm += 1
            s.unit_mm[u] = s.val("t_qkv")
            if fc == 2:
                s.tile_mm_done[(hp, pp)] = s.val("t_qkv")

        def emit_epi(u):
            pp, fc = u
            b = pp % 2
            t0 = pp * PAIR
            pbank = qbanks[(3 * pp + fc) % 4]
            s.wait("vector", "t_qkv", s.unit_mm[u])
            if s.ep_free is not None:
                s.wait("vector", s.ep_free[0], s.ep_free[1])
            s.run("vector", lambda v, pbank=pbank: v.tensor_copy(
                ep_f32[:, :], pbank[0:128, 0:PAIR]), inc="v_ep")
            v_ep_now = s.val("v_ep")
            s.bank_free[("ab", (3 * pp + fc) % 4)] = ("v_ep", v_ep_now)

            if fc < 2:
                s.wait("tensor", "v_ep", v_ep_now)
                if s.pb7_free is not None:
                    s.wait("tensor", s.pb7_free[0], s.pb7_free[1])
                s.run("tensor", lambda t: t.matmul(
                    pb[7][0:128, 0:PAIR], swapmat[:, :], ep_f32[:, :],
                    start=True, stop=True), inc="t_rp")
                t_rp_now = s.val("t_rp")
                s.wait("vector", "t_rp", t_rp_now)
                s.wait("vector", f"d_x{b}", s.pair_ld[pp])
                s.run("vector", lambda v, b=b: v.tensor_mul(
                    rp_t1[:, :], ep_f32[:, :], cttile[b][:, :]), inc="v_ep")
                s.run("vector", lambda v, b=b: v.tensor_mul(
                    rp_t2[:, :], pb[7][0:128, 0:PAIR], sttile[b][:, :]), inc="v_ep")
                s.wait("vector", "v_ep")
                s.pb7_free = ("v_ep", s.val("v_ep"))
                s.run("vector", lambda v: v.tensor_add(
                    ep_f32[:, :], rp_t1[:, :], rp_t2[:, :]), inc="v_ep")
                s.wait("vector", "v_ep")

            if fc == 0:
                if s.sp_free is not None:
                    s.wait("vector", s.sp_free[0], s.sp_free[1])
                s.run("vector", lambda v: v.tensor_copy(sp_hi[:, :], ep_f32[:, :]), inc="v_ep")
                s.wait("vector", "v_ep")
                s.run("vector", lambda v: v.tensor_tensor(
                    sp_lo[:, :], ep_f32[:, :], sp_hi[:, :], ALU.subtract), inc="v_ep")
                v_ep2 = s.val("v_ep")
                s.ep_free = ("v_ep", v_ep2)
                s.wait("scalar", "v_ep", v_ep2)
                s.run("scalar", lambda e, t0=t0: e.dma_start(
                    out=qsp_hi[:, t0:t0 + PAIR], in_=sp_hi[:, :]), inc="d_sp", n=16)
                s.run("scalar", lambda e, t0=t0: e.dma_start(
                    out=qsp_lo[:, t0:t0 + PAIR], in_=sp_lo[:, :]), inc="d_sp", n=16)
                s.sp_free = ("d_sp", s.val("d_sp"))
            elif fc == 1:
                s.run("vector", lambda v, t0=t0: v.tensor_copy(
                    k_hi[:, t0:t0 + PAIR], ep_f32[:, :]), inc="v_ep")
                s.wait("vector", "v_ep")
                s.run("vector", lambda v, t0=t0: v.tensor_tensor(
                    k_lo[:, t0:t0 + PAIR], ep_f32[:, :], k_hi[:, t0:t0 + PAIR],
                    ALU.subtract), inc="v_ep")
                s.ep_free = ("v_ep", s.val("v_ep"))
            else:
                for dc in range(KPP):
                    kci = t0 // 128 + dc
                    s.wait("tensor", "v_ep", v_ep_now)
                    if s.pb7_free is not None:
                        s.wait("tensor", s.pb7_free[0], s.pb7_free[1])
                    s.run("tensor", lambda t, dc=dc: t.transpose(
                        pb[7][0:128, 0:128], ep_f32[:, dc * 128:(dc + 1) * 128],
                        ident[:, :]), inc="t_rp")
                    t_rp_now = s.val("t_rp")
                    s.wait("vector", "t_rp", t_rp_now)
                    s.run("vector", lambda v, kci=kci: v.tensor_copy(
                        v_hi[:, kci * 128:(kci + 1) * 128], pb[7][0:128, 0:128]),
                        inc="v_ep")
                    s.wait("vector", "v_ep")
                    s.run("vector", lambda v, kci=kci: v.tensor_tensor(
                        v_lo[:, kci * 128:(kci + 1) * 128], pb[7][0:128, 0:128],
                        v_hi[:, kci * 128:(kci + 1) * 128], ALU.subtract), inc="v_ep")
                    s.pb7_free = ("v_ep", s.val("v_ep"))
                s.ep_free = ("t_rp", s.val("t_rp"))
            s.rope_done[(hp, pp)] = s.val("v_ep")

        units = [(pp, fc) for pp in range(NP) for fc in range(3)]
        for i in range(len(units) + 1):
            if i < len(units):
                emit_mm(units[i])
            if i >= 1:
                emit_epi(units[i - 1])
        for nm in ("d_x0", "d_x1", "d_wq"):
            s.wait("sync", nm)
        s.wait("scalar", "d_sp")

    LAG = 2
    def phaseB(hp):
        stb = [pb[0], pb[1], pb[4], pb[5]]
        ND = TQ // 128
        s.run("gpsimd", lambda g: g.memset(msk[:, :], 1.0), inc="g_mk")
        s.wait("gpsimd", "g_mk")
        for d in range(ND):
            s.run("gpsimd", lambda g, d=d: g.affine_select(
                msk[:, d * TQ:(d + 1) * TQ], msk[:, d * TQ:(d + 1) * TQ],
                pattern=[[1, TQ]], compare_op=ALU.is_ge, fill=0.0,
                base=-d * 128, channel_multiplier=-1), inc="g_mk")
        g_mk_now = s.val("g_mk")
        s.wait("vector", "g_mk", g_mk_now)
        for qt in range(NQ):
            q0 = qt * TQ
            nk = (q0 + TQ) // 128
            s.wait("tensor", "s_exp", s.val("s_exp"))
            s.wait("sync", "t_s", s.val("t_s"))
            voo_prev = s.val("v_oo")
            s.run("sync", lambda e, q0=q0: e.dma_start(
                out=q_hi[:, :], in_=qsp_hi[:, q0:q0 + TQ]), inc="d_q", n=16)
            s.run("sync", lambda e, q0=q0: e.dma_start(
                out=q_lo[:, :], in_=qsp_lo[:, q0:q0 + TQ]), inc="d_q", n=16)
            d_q_now = s.val("d_q")
            s.wait("tensor", "d_q", d_q_now)
            exp_done, av_done, ph_done, pl_done = {}, {}, {}, {}
            for step in range(nk + LAG):
                kc = step
                if kc < nk:
                    k0 = kc * 128
                    stp = stb[kc % 4]
                    if kc >= 4:
                        s.wait("tensor", "s_exp", exp_done[kc - 4])
                    nmm = 0
                    for (ksrc, qsrc) in ((k_hi, q_hi), (k_hi, q_lo), (k_lo, q_hi)):
                        s.run("tensor", lambda t, ksrc=ksrc, qsrc=qsrc, k0=k0, stp=stp,
                              first=(nmm == 0), last=(nmm == 2): t.matmul(
                            stp[0:128, 0:TQ], ksrc[:, k0:k0 + 128], qsrc[:, :],
                            start=first, stop=last),
                            inc="t_s" if nmm == 2 else None)
                        nmm += 1
                j = step - LAG
                if j < 0 or j >= nk:
                    continue
                k0 = j * 128
                bb = j % 3
                stpj = stb[j % 4]
                s.wait("scalar", "t_s", s.ts_base + (j + 1))
                if j >= 3:
                    s.wait("scalar", "v_ph", ph_done[j - 3])
                    s.wait("scalar", "g_pl", pl_done[j - 3])
                s.run("scalar", lambda sc, stpj=stpj, bb=bb: sc.activation(
                    p_sb[bb][:, :], stpj[0:128, 0:TQ], AF.Exp,
                    bias=shift_col[:, 0:1], scale=INV_SQRT_D), inc="s_exp")
                exp_done[j] = s.val("s_exp")
                s.wait("vector", "s_exp", exp_done[j])
                if k0 >= q0:
                    d = (k0 - q0) // 128
                    s.run("vector", lambda v, bb=bb, d=d: v.tensor_mul(
                        p_sb[bb][:, :], p_sb[bb][:, :], msk[:, d * TQ:(d + 1) * TQ]),
                        inc="v_mk")
                    s.wait("vector", "v_mk")
                if j >= 3:
                    s.wait("vector", "t_av", av_done[j - 3])
                s.run("vector", lambda v, bb=bb: v.tensor_copy(
                    p_hi[bb][:, :], p_sb[bb][:, :]), inc="v_ph")
                ph_done[j] = s.val("v_ph")
                s.wait("vector", "v_ph", ph_done[j])
                s.run("vector", lambda v, bb=bb: v.tensor_tensor(
                    p_lo[bb][:, :], p_sb[bb][:, :], p_hi[bb][:, :],
                    ALU.subtract), inc="g_pl")
                pl_done[j] = s.val("g_pl")
                s.wait("tensor", "v_ph", ph_done[j])
                s.wait("tensor", "g_pl", pl_done[j])
                if j == 0:
                    s.wait("tensor", "v_oo", voo_prev)
                s.run("tensor", lambda t, bb=bb, j=j: t.matmul(
                    pb[6][0:1, 0:TQ], ones_col_bf[:, 0:1], p_hi[bb][:, :],
                    start=(j == 0), stop=False))
                s.run("tensor", lambda t, bb=bb, j=j, nk=nk: t.matmul(
                    pb[6][0:1, 0:TQ], ones_col_bf[:, 0:1], p_lo[bb][:, :],
                    start=False, stop=(j == nk - 1)), inc="t_sum")
                nmm = 0
                for (vsrc, psrc) in ((v_hi, p_hi[bb]), (v_hi, p_lo[bb]), (v_lo, p_hi[bb])):
                    s.run("tensor", lambda t, vsrc=vsrc, psrc=psrc, k0=k0, j=j, nk=nk,
                          first=(j == 0 and nmm == 0),
                          last=(j == nk - 1 and nmm == 2): t.matmul(
                        pb[7][0:128, 0:TQ], vsrc[:, k0:k0 + 128], psrc[:, :],
                        start=first, stop=last),
                        inc="t_av" if nmm == 2 else None)
                    nmm += 1
                av_done[j] = s.val("t_av")
            s.ts_base = s.val("t_s")
            t_av_fin = s.val("t_av")
            t_sum_fin = s.val("t_sum")
            s.wait("vector", "t_sum", t_sum_fin)
            s.run("vector", lambda v: v.tensor_copy(row(0, TQ), pb[6][0:1, 0:TQ]), inc="v_oo")
            s.wait("vector", "v_oo")
            s.run("vector", lambda v: v.reciprocal(row(1, TQ), row(0, TQ)), inc="v_oo")
            v_oo_now = s.val("v_oo")
            s.wait("tensor", "v_oo", v_oo_now)
            s.run("tensor", lambda t: t.matmul(
                pb[6][0:128, 0:TQ], ones_row_f32[:, :], row(1, TQ),
                start=True, stop=True), inc="t_rb")
            t_rb_now = s.val("t_rb")
            s.wait("vector", "t_rb", t_rb_now)
            s.run("vector", lambda v: v.tensor_copy(rb_sb[:, :], pb[6][0:128, 0:TQ]), inc="v_oo")
            s.wait("vector", "t_av", t_av_fin)
            s.wait("vector", "v_oo")
            if s.o_free is not None:
                s.wait("vector", s.o_free[0], s.o_free[1])
            s.run("vector", lambda v: v.tensor_mul(
                o_sb[:, :], pb[7][0:128, 0:TQ], rb_sb[:, :]), inc="v_oo")
            v_oo2 = s.val("v_oo")
            s.wait("sync", "v_oo", v_oo2)
            s.run("sync", lambda e, hp=hp, q0=q0: e.dma_start(
                out=agin[hp].ap()[:, q0:q0 + TQ], in_=o_sb[:, :]),
                inc="d_o", n=16)
            s.o_free = ("d_o", s.val("d_o"))
        for nm in ("d_o", "d_q"):
            s.wait("sync", nm)
        s.wait("gpsimd", "d_o")
        s.run("gpsimd", lambda g, hp=hp: g.collective_compute(
            "AllGather", ALU.bypass, replica_groups=[list(range(cfg.NC))],
            ins=[agin[hp].ap().opt()], outs=[agout[hp].ap().opt()]), inc="g_ag")

    def row_math_c(div):
        s.run("vector", lambda v: v.tensor_scalar_mul(row(0, TF), pb[4][0:1, 0:TF], 1.0 / div), inc="v_rw")
        s.run("vector", lambda v: v.tensor_scalar_mul(row(1, TF), pb[5][0:1, 0:TF], 1.0 / div), inc="v_rw")
        s.wait("vector", "v_rw")
        s.run("vector", lambda v: v.tensor_mul(row(2, TF), row(0, TF), row(0, TF)), inc="v_rw")
        s.wait("vector", "v_rw")
        s.run("vector", lambda v: v.tensor_sub(row(3, TF), row(1, TF), row(2, TF)), inc="v_rw")
        s.wait("vector", "v_rw")
        s.run("vector", lambda v: v.tensor_scalar_add(row(3, TF), row(3, TF), LN_EPS), inc="v_rw")
        s.wait("vector", "v_rw")
        s.run("vector", lambda v: v.reciprocal(row(2, TF), row(3, TF)), inc="v_rw")
        s.wait("scalar", "v_rw")
        s.run("scalar", lambda sc: sc.activation(
            row(2, TF), row(2, TF), AF.Sqrt, bias=0.0, scale=1.0), inc="s_rw")
        s.wait("vector", "s_rw")
        s.run("vector", lambda v: v.tensor_mul(row(3, TF), row(0, TF), row(2, TF)), inc="v_rw")
        s.wait("vector", "v_rw")
        s.run("vector", lambda v: v.tensor_scalar_mul(row(3, TF), row(3, TF), -1.0), inc="v_rw")
        v_rw_now = s.val("v_rw")
        s.wait("tensor", "v_rw", v_rw_now)
        s.run("tensor", lambda t: t.matmul(
            pb[4][0:128, 0:TF], ones_row_f32[:, :], row(2, TF), start=True, stop=True), inc="t_bc")
        s.run("tensor", lambda t: t.matmul(
            pb[5][0:128, 0:TF], ones_row_f32[:, :], row(3, TF), start=True, stop=True), inc="t_bc")
        t_bc_now = s.val("t_bc")
        s.wait("vector", "t_bc", t_bc_now)
        s.run("vector", lambda v: v.tensor_copy(bCR[:, :], pb[4][0:128, 0:TF]), inc="v_bc")
        s.run("vector", lambda v: v.tensor_copy(bCM[:, :], pb[5][0:128, 0:TF]), inc="v_bc")
        s.wait("vector", "v_bc")

    def _c_stat_chunk(srct, ec, nchunks, ready=None):
        if ready is None:
            ready = ("v_pj", s.pj_ev_done[ec])
        if ec == 0:
            s.wait("tensor", "v_bc", s.val("v_bc"))
        s.wait("tensor", ready[0], ready[1])
        s.run("tensor", lambda t, ec=ec, first=(ec == 0), last=(ec == nchunks - 1): t.matmul(
            pb[4][0:1, 0:TF], ones_col_f32[:, :], srct[:, ec * TF:(ec + 1) * TF],
            start=first, stop=last), inc="t_sc")
        buf = csA[ec % 2]
        s.wait("vector", ready[0], ready[1])
        if ec >= 2:
            s.wait("vector", "t_sq", s.csq_done.get(ec - 2, 0))
        s.run("vector", lambda v, ec=ec, buf=buf: v.tensor_mul(
            buf[:, :], srct[:, ec * TF:(ec + 1) * TF], srct[:, ec * TF:(ec + 1) * TF]),
            inc="v_sq")
        v_sq_now = s.val("v_sq")
        s.wait("tensor", "v_sq", v_sq_now)
        s.run("tensor", lambda t, ec=ec, buf=buf, first=(ec == 0), last=(ec == nchunks - 1): t.matmul(
            pb[5][0:1, 0:TF], ones_col_f32[:, :], buf[:, :],
            start=first, stop=last), inc="t_sq")
        s.csq_done[ec] = s.val("t_sq")

    def _c_stat_rows(div):
        s.wait("vector", "t_sc", s.val("t_sc"))
        s.wait("vector", "t_sq", s.val("t_sq"))
        row_math_c(div)

    def stats_c(src, nchunks, div):
        s.wait("tensor", "v_bc", s.val("v_bc"))
        for ec in range(nchunks):
            s.run("tensor", lambda t, ec=ec, first=(ec == 0), last=(ec == nchunks - 1): t.matmul(
                pb[4][0:1, 0:TF], ones_col_f32[:, :], src[:, ec * TF:(ec + 1) * TF],
                start=first, stop=last),
                inc="t_sc" if ec == nchunks - 1 else None)
        sq_done = {}
        for ec in range(nchunks):
            buf = csA[ec % 2]
            if ec >= 2:
                s.wait("vector", "t_sq", sq_done[ec - 2])
            s.run("vector", lambda v, ec=ec, buf=buf: v.tensor_mul(
                buf[:, :], src[:, ec * TF:(ec + 1) * TF], src[:, ec * TF:(ec + 1) * TF]),
                inc="v_sq")
            v_sq_now = s.val("v_sq")
            s.wait("tensor", "v_sq", v_sq_now)
            s.run("tensor", lambda t, ec=ec, buf=buf, first=(ec == 0), last=(ec == nchunks - 1): t.matmul(
                pb[5][0:1, 0:TF], ones_col_f32[:, :], buf[:, :],
                start=first, stop=last), inc="t_sq")
            sq_done[ec] = s.val("t_sq")
        s.wait("vector", "t_sc", s.val("t_sc"))
        s.wait("vector", "t_sq", s.val("t_sq"))
        row_math_c(div)

    def phaseC():
        s.barrier("ph_c")

        for tf in range(NF):
            tb = tf * TF
            if tf > 0:
                s.wait("sync", "t_pj", s.val("t_pj"))
                s.wait("sync", "t_gu", s.val("t_gu"))
                s.wait("sync", "t_dn", s.val("t_dn"))
                s.wait("vector", "t_pj", s.val("t_pj"))
                s.wait("vector", "t_dn", s.val("t_dn"))
                s.wait("vector", "t_gu", s.val("t_gu"))
                s.wait("vector", "d_out", s.val("d_out"))
                s.wait("sync", "v_ot", s.val("v_ot"))
            HC2 = HC // 2
            s.wait("sync", "g_ag", 1)
            s.run("sync", lambda e, tf=tf, HC2=HC2: e.dma_start(
                out=y_f32[:, 0:HC2 * TF],
                in_=agout[0].ap().rearrange("(hc p) t -> p hc t", p=128)[
                    :, :, bass.ts(e.bass.partition_id([mybir.EngineType.SP]) * NF + tf, TF)]),
                inc="d_y0", n=16)
            d_y0_now = s.val("d_y0")
            s.wait("sync", "g_ag", 2)
            s.run("sync", lambda e, tf=tf, HC2=HC2: e.dma_start(
                out=y_f32[:, HC2 * TF:HC * TF],
                in_=agout[1].ap().rearrange("(hc p) t -> p hc t", p=128)[
                    :, :, bass.ts(e.bass.partition_id([mybir.EngineType.SP]) * NF + tf, TF)]),
                inc="d_y", n=16)
            d_y_now = s.val("d_y")
            s.wait("vector", "d_y0", d_y0_now)
            s.run("vector", lambda v: v.tensor_copy(
                y_hi[:, 0:HC2 * TF], y_f32[:, 0:HC2 * TF]), inc="v_y0")
            s.wait("vector", "v_y0")
            s.run("vector", lambda v: v.tensor_tensor(
                y_lo[:, 0:HC2 * TF], y_f32[:, 0:HC2 * TF], y_hi[:, 0:HC2 * TF],
                ALU.subtract), inc="v_y0")
            v_y0_now = s.val("v_y0")
            s.wait("vector", "d_y", d_y_now)
            s.run("vector", lambda v: v.tensor_copy(
                y_hi[:, HC2 * TF:], y_f32[:, HC2 * TF:HC * TF]), inc="v_y")
            s.wait("vector", "v_y")
            s.run("vector", lambda v: v.tensor_tensor(
                y_lo[:, HC2 * TF:], y_f32[:, HC2 * TF:HC * TF], y_hi[:, HC2 * TF:],
                ALU.subtract), inc="v_y")
            v_y_now = s.val("v_y")
            s.run("sync", lambda e, tb=tb: e.dma_start(
                out=x2[:, :],
                in_=x_own.rearrange("(ec p) t -> p ec t", p=128)[:, :, tb:tb + TF]),
                inc="d_x2", n=16)
            d_x2_now = s.val("d_x2")

            for fc in range(EC):
                wb = fc % 2
                if fc >= 2:
                    s.wait("sync", "v_w", s.wp_split_done.get(fc - 2, 0))
                s.run("sync", lambda e, fc=fc, wb=wb: e.dma_start(
                    out=wp_f32[wb][:, :],
                    in_=wprojT.rearrange("(hc p) f -> p hc f", p=128)[:, :, fc * 128:(fc + 1) * 128]),
                    inc=f"d_w{wb}", n=16)
                d_w_now = s.val(f"d_w{wb}")
                s.wait("vector", f"d_w{wb}", d_w_now)
                if fc >= 2:
                    s.wait("vector", "t_pj", s.pj_mm_done.get((tf, fc - 2), 0))
                s.run("vector", lambda v, wb=wb: v.tensor_copy(
                    wp_hi[wb][:, :], wp_f32[wb][:, :]), inc="v_w")
                s.wait("vector", "v_w")
                s.run("vector", lambda v, wb=wb: v.tensor_tensor(
                    wp_lo[wb][:, :], wp_f32[wb][:, :], wp_hi[wb][:, :], ALU.subtract), inc="v_w")
                s.wp_split_done[fc] = s.val("v_w")
                s.wait("tensor", "v_w", s.wp_split_done[fc])
                s.wait("tensor", "v_y0", v_y0_now)
                pbank = pb[fc % 2]
                s.wait("tensor", "v_pj", s.val("v_pj"))
                nmm = 0
                NMM = 3 * HC
                for ec in range(HC):
                    if ec == HC2:
                        s.wait("tensor", "v_y", v_y_now)
                    for (wsrc, ysrc) in ((wp_hi[wb], y_hi), (wp_hi[wb], y_lo), (wp_lo[wb], y_hi)):
                        s.run("tensor", lambda t, wsrc=wsrc, ysrc=ysrc, ec=ec, pbank=pbank,
                              first=(nmm == 0), last=(nmm == NMM - 1): t.matmul(
                            pbank[0:128, 0:TF],
                            wsrc[:, ec * 128:(ec + 1) * 128],
                            ysrc[:, ec * TF:(ec + 1) * TF],
                            start=first, stop=last),
                            inc="t_pj" if nmm == NMM - 1 else None)
                        nmm += 1
                s.pj_mm_done[(tf, fc)] = s.val("t_pj")
                s.wait("vector", "t_pj", s.pj_mm_done[(tf, fc)])
                s.run("vector", lambda v, fc=fc, pbank=pbank: v.tensor_copy(
                    ao[:, fc * TF:(fc + 1) * TF], pbank[0:128, 0:TF]), inc="v_pj")
                s.pj_ev_done[fc] = s.val("v_pj")
                if fc >= 1:
                    _c_stat_chunk(ao, fc - 1, EC)
            _c_stat_chunk(ao, EC - 1, EC)
            v_pj_now = s.val("v_pj")
            s.wait("tensor", "v_pj", v_pj_now)
            s.wait("vector", "v_pj", v_pj_now)
            _c_stat_rows(E)
            s.wait("vector", "d_x2", d_x2_now)
            s.run("vector", lambda v: v.tensor_tensor(
                ao[:, 0:EC * TF].rearrange("p (ec t) -> p ec t", t=TF),
                ao[:, 0:EC * TF].rearrange("p (ec t) -> p ec t", t=TF),
                bCR[:, :].rearrange("p (a t) -> p a t", a=1).broadcast_to((128, EC, TF)), ALU.mult), inc="v_x2")
            s.wait("vector", "v_x2")
            s.run("vector", lambda v: v.tensor_tensor(
                ao[:, 0:EC * TF].rearrange("p (ec t) -> p ec t", t=TF),
                ao[:, 0:EC * TF].rearrange("p (ec t) -> p ec t", t=TF),
                bCM[:, :].rearrange("p (a t) -> p a t", a=1).broadcast_to((128, EC, TF)), ALU.add), inc="v_x2")
            s.wait("vector", "v_x2")
            s.run("vector", lambda v: v.tensor_add(
                x2[:, :], x2[:, :], ao[:, 0:EC * TF]), inc="v_x2")
            v_x2_now = s.val("v_x2")
            for ec in range(EC):
                _c_stat_chunk(x2, ec, EC, ready=("v_x2", v_x2_now))
            s.wait("tensor", "v_x2", v_x2_now)
            _c_stat_rows(E)
            s.wait("vector", "t_gu", s.val("t_gu"))
            s.run("vector", lambda v: v.tensor_tensor(
                ao[:, 0:EC * TF].rearrange("p (ec t) -> p ec t", t=TF),
                x2[:, :].rearrange("p (ec t) -> p ec t", t=TF),
                bCR[:, :].rearrange("p (a t) -> p a t", a=1).broadcast_to((128, EC, TF)), ALU.mult), inc="v_l2")
            s.wait("vector", "v_l2")
            s.run("vector", lambda v: v.tensor_tensor(
                ao[:, 0:EC * TF].rearrange("p (ec t) -> p ec t", t=TF),
                ao[:, 0:EC * TF].rearrange("p (ec t) -> p ec t", t=TF),
                bCM[:, :].rearrange("p (a t) -> p a t", a=1).broadcast_to((128, EC, TF)), ALU.add), inc="v_l2")
            s.wait("vector", "v_l2")
            s.run("vector", lambda v: v.tensor_copy(x2h[:, :], ao[:, 0:EC * TF]), inc="v_l2")
            s.wait("vector", "v_l2")
            s.run("vector", lambda v: v.tensor_tensor(
                x2l[:, :], ao[:, 0:EC * TF], x2h[:, :], ALU.subtract), inc="v_l2")
            v_l2_now = s.val("v_l2")

            s.wait("gpsimd", "v_l2", v_l2_now)
            for ic in range(IC):
                gb = ic % 2
                if ic >= 2:
                    s.wait("gpsimd", "t_gu", s.gu_mm_done.get((tf, ic - 2), 0))
                s.run("gpsimd", lambda g, ic=ic, gb=gb: g.dma_start(
                    out=gu_bf[gb][:, 0:EC * 128].rearrange("p (ec i) -> p ec i", i=128),
                    in_=gateT.rearrange("(ec p) i -> p ec i", p=128)[:, :, ic * 128:(ic + 1) * 128]),
                    inc=f"d_g{gb}", n=16)
                s.run("gpsimd", lambda g, ic=ic, gb=gb: g.dma_start(
                    out=gu_bf[gb][:, EC * 128:2 * EC * 128].rearrange("p (ec i) -> p ec i", i=128),
                    in_=upT.rearrange("(ec p) i -> p ec i", p=128)[:, :, ic * 128:(ic + 1) * 128]),
                    inc=f"d_g{gb}", n=16)
                d_g_now = s.val(f"d_g{gb}")
                s.wait("tensor", f"d_g{gb}", d_g_now)
                s.wait("tensor", "v_l2", v_l2_now)
                gpb = pb[4 + gb]
                upb = pb[0 + gb]
                if ic >= 2:
                    s.wait("tensor", "s_fe", s.fe_s_done.get((tf, ic - 2), 0))
                    s.wait("tensor", "v_fe", s.fe_v1_done.get((tf, ic - 2), 0))
                nmm = 0
                NMM = 2 * EC
                for ec in range(EC):
                    for xsrc in (x2h, x2l):
                        s.run("tensor", lambda t, ec=ec, gb=gb, xsrc=xsrc, gpb=gpb,
                              first=(nmm == 0), last=(nmm == NMM - 1): t.matmul(
                            gpb[0:128, 0:TF],
                            gu_bf[gb][:, ec * 128:(ec + 1) * 128],
                            xsrc[:, ec * TF:(ec + 1) * TF],
                            start=first, stop=last),
                            inc="t_gu" if nmm == NMM - 1 else None)
                        nmm += 1
                nmm = 0
                for ec in range(EC):
                    for xsrc in (x2h, x2l):
                        s.run("tensor", lambda t, ec=ec, gb=gb, xsrc=xsrc, upb=upb,
                              first=(nmm == 0), last=(nmm == NMM - 1): t.matmul(
                            upb[0:128, 0:TF],
                            gu_bf[gb][:, EC * 128 + ec * 128:EC * 128 + (ec + 1) * 128],
                            xsrc[:, ec * TF:(ec + 1) * TF],
                            start=first, stop=last),
                            inc="t_gu" if nmm == NMM - 1 else None)
                        nmm += 1
                s.gu_mm_done[(tf, ic)] = s.val("t_gu")
                s.wait("scalar", "t_gu", s.gu_mm_done[(tf, ic)])
                if ic >= 2:
                    s.wait("scalar", "v_fe", s.fe_v1_done.get((tf, ic - 2), 0))
                else:
                    s.wait("scalar", "t_sq", s.val("t_sq"))
                s.run("scalar", lambda sc, ic=ic, gpb=gpb, gb=gb: sc.activation(
                    csA[gb][:, :], gpb[0:128, 0:TF], AF.Silu, bias=0.0,
                    scale=gs_cols[:, ic:ic + 1]), inc="s_fe")
                s.fe_s_done[(tf, ic)] = s.val("s_fe")
                s.wait("vector", "s_fe", s.fe_s_done[(tf, ic)])
                if ic >= 2:
                    s.wait("vector", "v_fe", s.fe_v2_done.get((tf, ic - 2), 0))
                s.run("vector", lambda v, gb=gb, upb=upb: v.tensor_mul(
                    csB[gb][:, :], csA[gb][:, :], upb[0:128, 0:TF]), inc="v_fe")
                s.fe_v1_done[(tf, ic)] = s.val("v_fe")
                s.wait("vector", "v_fe")
                s.run("vector", lambda v, ic=ic, gb=gb: v.tensor_scalar(
                    csB[gb][:, :], csB[gb][:, :], us_cols[:, ic:ic + 1], None, ALU.mult), inc="v_fe")
                s.wait("vector", "v_fe")
                s.run("vector", lambda v, ic=ic, gb=gb: v.tensor_copy(
                    h_hi[:, ic * TF:(ic + 1) * TF], csB[gb][:, :]), inc="v_fe")
                s.wait("vector", "v_fe")
                s.run("vector", lambda v, ic=ic, gb=gb: v.tensor_tensor(
                    h_lo[:, ic * TF:(ic + 1) * TF], csB[gb][:, :],
                    h_hi[:, ic * TF:(ic + 1) * TF], ALU.subtract), inc="v_fe")
                s.fe_v2_done[(tf, ic)] = s.val("v_fe")
            v_fe_all = s.val("v_fe")

            s.wait("gpsimd", "t_gu", s.val("t_gu"))
            for hc in range(EC):
                db = hc % 2
                if hc >= 2:
                    s.wait("gpsimd", "t_dn", s.dn_mm_done.get((tf, hc - 2), 0))
                s.run("gpsimd", lambda g, hc=hc, db=db: g.dma_start(
                    out=dn_bf[db][:, :].rearrange("p (ic h) -> p ic h", h=128),
                    in_=downT.rearrange("(ic p) h -> p ic h", p=128)[:, :, hc * 128:(hc + 1) * 128]),
                    inc=f"d_d{db}", n=16)
                d_d_now = s.val(f"d_d{db}")
                s.wait("tensor", f"d_d{db}", d_d_now)
                s.wait("tensor", "v_fe", v_fe_all)
                pbank = pb[6 + hc % 2]
                s.wait("tensor", "v_dn", s.val("v_dn"))
                nmm = 0
                NMM = 2 * IC
                for ic in range(IC):
                    for hsrc in (h_hi, h_lo):
                        s.run("tensor", lambda t, ic=ic, db=db, hsrc=hsrc, pbank=pbank,
                              first=(nmm == 0), last=(nmm == NMM - 1): t.matmul(
                            pbank[0:128, 0:TF],
                            dn_bf[db][:, ic * 128:(ic + 1) * 128],
                            hsrc[:, ic * TF:(ic + 1) * TF],
                            start=first, stop=last),
                            inc="t_dn" if nmm == NMM - 1 else None)
                        nmm += 1
                s.dn_mm_done[(tf, hc)] = s.val("t_dn")
                s.wait("vector", "t_dn", s.dn_mm_done[(tf, hc)])
                s.run("vector", lambda v, hc=hc, pbank=pbank: v.tensor_scalar(
                    oo[:, hc * TF:(hc + 1) * TF], pbank[0:128, 0:TF],
                    ds_cols[:, hc:hc + 1], None, ALU.mult), inc="v_dn")
                s.dn_ev_done[hc] = s.val("v_dn")
                if hc >= 1:
                    _c_stat_chunk(oo, hc - 1, EC, ready=("v_dn", s.dn_ev_done[hc - 1]))
            _c_stat_chunk(oo, EC - 1, EC, ready=("v_dn", s.dn_ev_done[EC - 1]))
            v_dn_now = s.val("v_dn")
            s.wait("tensor", "v_dn", v_dn_now)
            s.wait("vector", "v_dn", v_dn_now)
            _c_stat_rows(E)
            s.run("vector", lambda v: v.tensor_tensor(
                oo[:, :].rearrange("p (ec t) -> p ec t", t=TF),
                oo[:, :].rearrange("p (ec t) -> p ec t", t=TF),
                bCR[:, :].rearrange("p (a t) -> p a t", a=1).broadcast_to((128, EC, TF)), ALU.mult), inc="v_ot")
            s.wait("vector", "v_ot")
            s.run("vector", lambda v: v.tensor_tensor(
                oo[:, :].rearrange("p (ec t) -> p ec t", t=TF),
                oo[:, :].rearrange("p (ec t) -> p ec t", t=TF),
                bCM[:, :].rearrange("p (a t) -> p a t", a=1).broadcast_to((128, EC, TF)), ALU.add), inc="v_ot")
            s.wait("vector", "v_ot")
            s.run("vector", lambda v: v.tensor_add(oo[:, :], x2[:, :], oo[:, :]), inc="v_ot")
            s.wait("vector", "v_ot")
            v_ot_now = s.val("v_ot")
            s.wait("sync", "v_ot", v_ot_now)
            s.run("sync", lambda e, tb=tb: e.dma_start(
                out=out_fm.rearrange("(ec p) t -> p ec t", p=128)[:, :, tb:tb + TF],
                in_=oo[:, :]), inc="d_out", n=16)
        s.wait("sync", "d_out")

    s.tile_mm_done = {}
    s.ts_base = 0
    s.xa_done = {}
    s.xa_tdone = {}
    s.rw_done = {}
    s.bc_done = {}
    s.vbc_done = {}
    s.sq_done = {}
    s.xa_ld = {}
    s.sxc_done = {}
    s.srw_done = {}
    s.rwA_done = {}
    s.rwB_done = {}
    s.ls_done = {}
    s.rs_done = {}
    s.bcld_done = {}
    s.rope_done = {}
    s.pair_ld = {}
    s.unit_mm = {}
    s.gsp_done = {}
    s.bank_free = {}
    s.ep_free = None
    s.pb7_free = None
    s.sp_free = None
    s.o_free = None
    s.lnspill_done = {}
    s.wp_split_done = {}
    s.pj_mm_done = {}
    s.gu_mm_done = {}
    s.dn_mm_done = {}
    s.pj_ev_done = {}
    s.dn_ev_done = {}
    s.csq_done = {}
    s.fe_s_done = {}
    s.fe_v1_done = {}
    s.fe_v2_done = {}

    s.mark("ph0")
    phase0()
    s.barrier("ph0")
    s.mark("A0pre")
    phaseA0()
    s.barrier("phA0pre")
    for hp in range(cfg.HPC):
        s.mark(f"A{hp}")
        phaseA(hp)
        s.barrier(f"phA{hp}")
        s.mark(f"B{hp}")
        phaseB(hp)
        s.barrier(f"phB{hp}")
    s.mark("C")
    phaseC()

    block = nc.Block("main").__enter__()
    block.sync(lambda e: s.replay("sync", e))
    block.tensor(lambda e: s.replay("tensor", e))
    block.vector(lambda e: s.replay("vector", e))
    block.scalar(lambda e: s.replay("scalar", e))
    block.gpsimd(lambda e: s.replay("gpsimd", e))
    block.__exit__(None, None, None)
    return nc


def _rope_perm():
    return np.concatenate([np.arange(0, 128, 2), np.arange(1, 128, 2)])


def prepare_inputs(cfg, x, Wqkv, Wproj, gw, uw, dw, gs, us, ds):
    T, E, I, NC, D = cfg.T, cfg.E, cfg.I, cfg.NC, cfg.D
    QT = cfg.HD
    x = np.asarray(x, dtype=np.float32).reshape(T, E)
    Wqkv = np.asarray(Wqkv, dtype=np.float32)
    Wproj = np.asarray(Wproj, dtype=np.float32)
    gw = np.asarray(gw, dtype=np.float32)
    uw = np.asarray(uw, dtype=np.float32)
    dw = np.asarray(dw, dtype=np.float32)
    gs = np.asarray(gs, dtype=np.float32)
    us = np.asarray(us, dtype=np.float32)
    ds = np.asarray(ds, dtype=np.float32)

    x_fm = np.ascontiguousarray(x.T)
    HC = QT // 128
    permw = np.concatenate([np.arange(0, HC, 2), np.arange(1, HC, 2)])
    wprojT = np.ascontiguousarray(
        Wproj.T.reshape(HC, 128, -1)[permw].reshape(QT, -1))
    gateT = np.ascontiguousarray(gw.T)
    upT = np.ascontiguousarray(uw.T)
    downT = np.ascontiguousarray(dw.T)
    gate_s = np.ascontiguousarray(gs.reshape(cfg.IC, 128).T)
    up_s = np.ascontiguousarray(us.reshape(cfg.IC, 128).T)
    down_s = np.ascontiguousarray(ds.reshape(cfg.EC, 128).T)

    perm = _rope_perm()
    in_maps = []
    for c in range(NC):
        cols = []
        for hp in range(cfg.HPC):
            h = cfg.HPC * c + hp
            q = Wqkv[h * D:(h + 1) * D][perm]
            k = Wqkv[QT + h * D:QT + (h + 1) * D][perm]
            v = Wqkv[2 * QT + h * D:2 * QT + (h + 1) * D]
            cols.append(np.concatenate([q, k, v], axis=0))
        wq = np.concatenate(cols, axis=0)
        wqkvT = np.ascontiguousarray(wq.T)
        x_own = np.ascontiguousarray(x_fm[:, c * cfg.To:(c + 1) * cfg.To])
        in_maps.append({
            "x_fm": x_fm, "x_own": x_own, "wqkvT": wqkvT, "wprojT": wprojT,
            "gateT": gateT, "upT": upT, "downT": downT,
            "gate_s": gate_s, "up_s": up_s, "down_s": down_s,
        })
    return in_maps


def kernel(L0_attn_x, L0_attn_W_qkv, L0_attn_W_proj, L0_ffn_gate_w,
           L0_ffn_up_w, L0_ffn_down_w, L0_ffn_gate_s, L0_ffn_up_s,
           L0_ffn_down_s):
    cfg = Cfg()
    in_maps = prepare_inputs(cfg, L0_attn_x, L0_attn_W_qkv, L0_attn_W_proj,
                             L0_ffn_gate_w, L0_ffn_up_w, L0_ffn_down_w,
                             L0_ffn_gate_s, L0_ffn_up_s, L0_ffn_down_s)
    nc = build_program(cfg)
    res = run_bass_kernel_spmd(nc, in_maps, core_ids=list(range(cfg.NC)))
    parts = [res.results[c]["out_fm"] for c in range(cfg.NC)]
    out = np.concatenate(parts, axis=1)
    return np.ascontiguousarray(out.T).reshape(1, cfg.T, cfg.E)
